# version 01 (final)
# baseline (speedup 1.0000x reference)
import numpy as np
import concourse.bass as bass
import concourse.mybir as mybir

F32 = mybir.dt.float32
F32R = mybir.dt.float32r
BF16 = mybir.dt.bfloat16
U32 = mybir.dt.uint32
AF = mybir.ActivationFunctionType
ALU = mybir.AluOpType
AXX = mybir.AxisListType.X

T, D, H, HD = 4096, 1024, 16, 64
NT = T // 128
ND = D // 128
NP = H // 2
CAP = 1024
E1, HID1 = 32, 512
E0, HID0 = 8, 2048
EPS = 1e-5
SCALE = HD ** -0.5

ENGINES = ("tensor", "vector", "scalar", "gpsimd", "sync")


class Sem:
    def __init__(self, handle, name):
        self.h = handle
        self.name = name
        self.n = 0


class K:
    def __init__(self, nc, ctx):
        self.nc = nc
        self.ctx = ctx
        self.q = {e: [] for e in ENGINES}
        self.sems = {}
        for name in ("LD", "GP", "SQ", "MM", "VE", "SC", "GC", "SY"):
            self.sems[name] = Sem(ctx.enter_context(nc.semaphore(name)), name)

    def s(self, name):
        return self.sems[name]

    def op(self, eng, fname, *a, waits=(), inc=None, inc_n=None, lowp=False, **kw):
        if inc is not None:
            if inc_n is None:
                inc_n = 16 if "dma" in fname else 1
            inc.n += inc_n
            post = inc.n
        else:
            post = None
        waits = tuple((s, v) for (s, v) in waits if v > 0)
        nc = self.nc

        import traceback
        site = "".join(traceback.format_stack(limit=3)[:-1])

        def fn(E, a=a, kw=kw, waits=waits, inc=inc, inc_n=inc_n, fname=fname, lowp=lowp, site=site):
            for s, v in waits:
                E.wait_ge(s.h, v)
            try:
                if lowp:
                    with nc.allow_low_precision(reason="intended f32r/bf16 rounding"):
                        ins = getattr(E, fname)(*a, **kw)
                else:
                    ins = getattr(E, fname)(*a, **kw)
            except Exception:
                info = []
                for kk, vv in kw.items():
                    try:
                        info.append(f"{kk}={getattr(vv, 'name', None) or ''}{getattr(vv, 'shape', '')} off={getattr(vv, 'offset', '')}")
                    except Exception:
                        info.append(f"{kk}=?")
                print("FAILED op", fname, "args:", "; ".join(str(x) for x in info), "emitted at:\n", site)
                raise
            if inc is not None:
                if ins is None:
                    ins = E.nop()
                ins.then_inc(inc.h, inc_n)
        self.q[eng].append(fn)
        return post

    def drain_pad(self, eng, scratch_ap):
        if eng == "vector":
            self.op(eng, "memset", scratch_ap, 0.0, inc=self.s("VE"))
        else:
            self.op(eng, "activation", out=scratch_ap, in_=scratch_ap,
                    func=AF.Copy, inc=self.s("SC"))

    def settle(self, eng, sem, n):
        self.wait(eng, sem, n)
        return self.op(eng, "nop", inc=sem)

    def wait(self, eng, sem, val):
        if val <= 0:
            return

        def fn(E, sem=sem, val=val):
            E.wait_ge(sem.h, val)
        self.q[eng].append(fn)

    def barrier(self):
        prog = {"tensor": "MM", "vector": "VE", "scalar": "SC",
                "gpsimd": "GC", "sync": "SY"}
        marks = {}
        for e in ENGINES:
            s = self.s(prog[e])
            marks[e] = self.op(e, "nop", inc=s)
        for e in ENGINES:
            for e2 in ENGINES:
                if e2 != e:
                    self.wait(e, self.s(prog[e2]), marks[e2])

    def finalize(self, block):
        for e in ENGINES:
            ops = self.q[e]
            getattr(block, e)(lambda E, ops=ops: [fn(E) for fn in ops])



def host_prepare(inp):
    f = np.ascontiguousarray
    d = {}
    d["l0_wqT"] = f(inp["L0_attn_Wq"].T)
    d["l0_wkT"] = f(inp["L0_attn_Wk"].T)
    d["l0_wvT"] = f(inp["L0_attn_Wv"].T)
    d["l0_woT"] = f(inp["L0_attn_Wo"].T)
    d["l0_bq"] = f(inp["L0_attn_bq"].reshape(ND, 128).T)
    d["l0_bk"] = f(inp["L0_attn_bk"].reshape(ND, 128).T)
    d["l0_bv_row"] = f(inp["L0_attn_bv"].reshape(1, D))
    d["l0_bo_row"] = f(inp["L0_attn_bo"].reshape(1, D))
    d["l0_arep"] = f(np.repeat(np.asarray(inp["L0_attn_adaptive_scale"]), HD).reshape(ND, 128).T)
    d["l0_gamma_row"] = f(inp["L0_attn_gamma"].reshape(1, D))
    d["l0_beta_row"] = f(inp["L0_attn_beta"].reshape(1, D))
    d["l0_rw"] = f(inp["L0_ffn_router_w"])
    d["l0_rb_row"] = f(inp["L0_ffn_router_b"].reshape(1, E0))
    d["l0_sw1"] = f(inp["L0_ffn_sw1"])
    d["l0_sb1"] = f(inp["L0_ffn_sb1"].reshape(HID0 // 128, 128).T)
    d["l0_sw2"] = f(inp["L0_ffn_sw2"])
    d["l0_sb2_row"] = f(inp["L0_ffn_sb2"].reshape(1, D))
    d["l0_ew1"] = f(inp["L0_ffn_rw1"])
    d["l0_eb1"] = f(np.transpose(inp["L0_ffn_rb1"].reshape(E0, HID0 // 128, 128), (0, 2, 1)))
    d["l0_ew2"] = f(inp["L0_ffn_rw2"])
    d["l0_eb2"] = f(inp["L0_ffn_rb2"].reshape(E0, 1, D))
    d["l1_wq"] = f(inp["L1_attn_Wq"])
    d["l1_bq"] = f(inp["L1_attn_bq"].reshape(ND, 128).T)
    d["l1_wl"] = f(inp["L1_attn_Wl"])
    d["l1_bl"] = f(inp["L1_attn_bl"].reshape(2, 128).T)
    d["l1_wk"] = f(inp["L1_attn_Wk"])
    d["l1_bk"] = f(inp["L1_attn_bk"].reshape(ND, 128).T)
    d["l1_wv"] = f(inp["L1_attn_Wv"])
    d["l1_bv_row"] = f(inp["L1_attn_bv"].reshape(1, D))
    d["l1_wo"] = f(inp["L1_attn_Wo"])
    d["l1_bo_row"] = f(inp["L1_attn_bo"].reshape(1, D))
    d["l1_rw0"] = f(inp["L1_ffn_router_w0"])
    d["l1_rw1"] = f(inp["L1_ffn_router_w1"])
    for g, pre in ((0, "0"), (1, "1")):
        d[f"l1_wg{pre}"] = f(inp[f"L1_ffn_wg{pre}"])
        d[f"l1_wu{pre}"] = f(inp[f"L1_ffn_wu{pre}"])
        d[f"l1_wd{pre}"] = f(inp[f"L1_ffn_wd{pre}"])
    d["ident"] = np.eye(128, dtype=np.float32)
    d["triu"] = np.triu(np.ones((128, 128), np.float32), 1)
    d["ones_col"] = np.ones((128, 1), np.float32)
    d["ones_row"] = np.ones((1, 128), np.float32)
    d["iota_row"] = np.arange(E1, dtype=np.float32).reshape(1, E1)
    d["ecap_row"] = (np.arange(E1, dtype=np.float32) * CAP).reshape(1, E1)
    cs = np.zeros((128, 8), np.float32)
    cs[:, 0] = EPS
    cs[:, 1] = 1e-30
    cs[:, 2] = SCALE
    cs[:, 3] = 1.0
    d["consts"] = cs
    return d



class MK:
    def __init__(self, nc, ctx, k, dbg=False):
        self.nc = nc
        self.ctx = ctx
        self.k = k
        self.dbg = dbg
        self.dram = {}
        self.fixed = {}

    def din(self, name, shape, dtype):
        self.dram[name] = self.nc.dram_tensor(name, list(shape), dtype, kind="ExternalInput").ap()
        return self.dram[name]

    def dscratch(self, name, shape, dtype):
        if name in getattr(self, "dbg_in", ()):
            kind = "ExternalInput"
        elif self.dbg:
            kind = "ExternalOutput"
        else:
            kind = "Internal"
        self.dram[name] = self.nc.dram_tensor(name, list(shape), dtype, kind=kind).ap()
        return self.dram[name]

    def dout(self, name, shape, dtype):
        self.dram[name] = self.nc.dram_tensor(name, list(shape), dtype, kind="ExternalOutput").ap()
        return self.dram[name]

    def sb(self, name, shape, dtype):
        ctx = getattr(self, "cur_ctx", None) or self.ctx
        t = ctx.enter_context(self.nc.sbuf_tensor(name, list(shape), dtype))
        return t

    def declare_dram(self):
        din = self.din
        din("x0", (T, D), F32)
        for n, sh in (("l0_wqT", (D, D)), ("l0_wkT", (D, D)), ("l0_wvT", (D, D)),
                      ("l0_woT", (D, D)), ("l0_rw", (D, E0)), ("l0_sw1", (D, HID0)),
                      ("l0_sw2", (HID0, D)), ("l0_ew1", (E0, D, HID0)),
                      ("l0_ew2", (E0, HID0, D)),
                      ("l1_wq", (D, D)), ("l1_wl", (D, 256)), ("l1_wk", (256, D)),
                      ("l1_wv", (256, D)), ("l1_wo", (D, D)),
                      ("l1_rw0", (D, E1)), ("l1_rw1", (D, E1))):
            din(n, sh, F32R)
        for n in ("l0_bq", "l0_bk", "l0_arep", "l0_sb1", "l1_bq", "l1_bl", "l1_bk", "consts"):
            din(n, (128, {"l0_sb1": 16, "l1_bl": 2, "consts": 8}.get(n, ND)), F32)
        for n in ("l0_bv_row", "l0_bo_row", "l0_gamma_row", "l0_beta_row", "l0_sb2_row",
                  "l1_bv_row", "l1_bo_row"):
            din(n, (1, D), F32R)
        din("l0_rb_row", (1, E0), F32R)
        din("l0_eb1", (E0, 128, 16), F32)
        din("l0_eb2", (E0, 1, D), F32R)
        for g in (0, 1):
            din(f"l1_wg{g}", (E1, D, HID1), F32R)
            din(f"l1_wu{g}", (E1, D, HID1), F32R)
            din(f"l1_wd{g}", (E1, HID1, D), F32R)
        for n in ("ident", "triu"):
            din(n, (128, 128), F32R)
        din("ones_col", (128, 1), F32R)
        din("ones_row", (1, 128), F32R)
        din("iota_row", (1, E1), F32R)
        din("ecap_row", (1, E1), F32R)
        ds = self.dscratch
        ds("xnT", (D, T), F32R)
        ds("xn_tok", (T, D), F32)
        ds("qT", (D, T), F32R)
        ds("kT", (D, T), F32R)
        ds("vtok", (T, 1040), F32R)
        ds("sacc", (T, D), F32)
        ds("sfin", (T, D), F32)
        ds("gacc", (T, D), F32)
        ds("x1", (T, D), F32)
        for g in (0, 1):
            ds(f"cap{g}", (E1 * CAP, D), BF16)
            ds(f"res{g}", (E1 * CAP, D), F32)
        self.dout("out", (T, D), F32)

    def load_fixed(self):
        k, sb = self.k, self.sb
        LD = k.s("LD")
        fx = self.fixed
        fx["ident"] = sb("fx_ident", (128, 128), F32R)
        fx["triu"] = sb("fx_triu", (128, 128), F32R)
        fx["ones_col"] = sb("fx_ones_col", (128, 1), F32R)
        fx["ones_row"] = sb("fx_ones_row", (1, 128), F32R)
        fx["iota_row"] = sb("fx_iota_row", (1, E1), F32R)
        fx["ecap_row"] = sb("fx_ecap_row", (1, E1), F32R)
        fx["consts"] = sb("fx_consts", (128, 8), F32)
        fx["W0"] = sb("fx_W0", (128, NT * E0), F32)
        fx["RBC"] = sb("fx_RBC", (128, E0), F32)
        fx["W10"] = sb("fx_W10", (128, NT * E1), F32)
        fx["W11"] = sb("fx_W11", (128, NT * E1), F32)
        fx["RT"] = sb("fx_RT", (128, 2 * NT * 4), F32)
        fx["SLOT"] = sb("fx_SLOT", (128, 2 * NT * 2), U32)
        fx["bias"] = sb("fx_bias", (128, 64), F32)
        fx["rows"] = sb("fx_rows", (1, 64), F32)
        for name in ("W0", "RT", "W10", "W11", "SLOT", "bias", "rows", "RBC"):
            k.op("vector", "memset", fx[name][:], 0, inc=k.s("VE"))
        n = 0
        for name in ("ident", "triu", "ones_col", "ones_row", "iota_row", "ecap_row", "consts"):
            n = k.op("sync", "dma_start", out=fx[name][:], in_=self.dram[name], inc=LD)
        self.fixed_loaded = n


def relerr(a, b):
    a = np.asarray(a, np.float64)
    b = np.asarray(b, np.float64)
    return np.linalg.norm(a - b) / (np.linalg.norm(b) + 1e-12)



def layer_prep(mk, L):
    k, nc, fx, dram = mk.k, mk.nc, mk.fixed, mk.dram
    LD, GP, MM, VE, SC = (k.s(x) for x in ("LD", "GP", "MM", "VE", "SC"))
    if "l0rw" not in fx:
        fx["l0rw"] = mk.sb("fx_l0rw", (128, E0 * ND), F32R)
        fx["l1rw"] = mk.sb("fx_l1rw", (128, 2 * E1 * ND), F32R)
        fx["IOT"] = mk.sb("fx_IOT", (128, E1), F32)
        fx["base"] = mk.sb("fx_base", (1, 64), F32R)
        fx["bcol"] = mk.sb("fx_bcol", (128, 48), F32)
        fx["rbrow"] = mk.sb("fx_rbrow", (1, E0), F32R)
    w = {}
    if L == 0:
        w["rw"] = k.op("sync", "dma_start",
                       out=fx["l0rw"][:].rearrange("p (c e) -> p c e", c=ND),
                       in_=dram["l0_rw"].rearrange("(c p) e -> p c e", p=128), inc=LD)
        w["rb"] = k.op("sync", "dma_start", out=fx["rbrow"][:], in_=dram["l0_rb_row"], inc=LD)
        nrb = k.op("tensor", "matmul", mk.PS[:, 8:16], fx["ones_row"][:], fx["rbrow"][:],
                   start=True, stop=True, waits=[(LD, w["rb"]), (LD, mk.fixed_loaded)], inc=MM)
        w["rbc"] = k.op("scalar", "activation", out=fx["RBC"][:], in_=mk.PS[:, 8:16],
                        func=AF.Copy, waits=[(MM, nrb)], inc=SC)
        k.op("sync", "dma_start", out=fx["bcol"][:, 0:8], in_=dram["l0_bq"], inc=LD)
        k.op("sync", "dma_start", out=fx["bcol"][:, 8:16], in_=dram["l0_bk"], inc=LD)
        n = k.op("sync", "dma_start", out=fx["bcol"][:, 16:24], in_=dram["l0_arep"], inc=LD)
        n1 = k.op("vector", "tensor_scalar", out=fx["bcol"][:, 24:32], in0=fx["bcol"][:, 16:24],
                  scalar1=SCALE, scalar2=None, op0=ALU.mult, waits=[(LD, n)], inc=VE)
        k.wait("vector", VE, n1)
        w["qs"] = k.op("vector", "tensor_tensor", out=fx["bcol"][:, 32:40], in0=fx["bcol"][:, 0:8],
                       in1=fx["bcol"][:, 24:32], op=ALU.mult, inc=VE)
    else:
        for g in (0, 1):
            w[f"rw{g}"] = k.op("sync", "dma_start",
                               out=fx["l1rw"][:, g * 256:(g + 1) * 256].rearrange("p (c e) -> p c e", c=ND),
                               in_=dram[f"l1_rw{g}"].rearrange("(c p) e -> p c e", p=128), inc=LD)
        k.op("sync", "dma_start", out=fx["bcol"][:, 0:8], in_=dram["l1_bq"], inc=LD)
        k.op("sync", "dma_start", out=fx["bcol"][:, 16:18], in_=dram["l1_bl"], inc=LD)
        n = k.op("sync", "dma_start", out=fx["bcol"][:, 8:16], in_=dram["l1_bk"], inc=LD)
        w["qs"] = k.op("vector", "tensor_scalar", out=fx["bcol"][:, 32:40], in0=fx["bcol"][:, 0:8],
                       scalar1=SCALE, scalar2=None, op0=ALU.mult, waits=[(LD, n)], inc=VE)
        if True:
            nm = k.op("tensor", "matmul", mk.PS[:, 0:E1], fx["ones_row"][:], fx["iota_row"][:],
                      start=True, stop=True, waits=[(LD, mk.fixed_loaded)], inc=MM)
            w["iot"] = k.op("scalar", "activation", out=fx["IOT"][:], in_=mk.PS[:, 0:E1],
                            func=AF.Copy, waits=[(MM, nm)], inc=SC)
            nz = k.op("vector", "memset", fx["rows"][0:1, 0:64], 0.0, inc=VE)
            w["base"] = k.op("scalar", "activation", out=fx["base"][:], in_=fx["rows"][0:1, 0:64],
                             func=AF.Copy, waits=[(VE, nz)], inc=SC)
    return w


def phase_A(mk, L, prep):
    k, nc, fx, dram = mk.k, mk.nc, mk.fixed, mk.dram
    LD, GP, MM, VE, SC = (k.s(x) for x in ("LD", "GP", "MM", "VE", "SC"))
    PS = mk.PS
    xsrc = dram["x0"] if L == 0 else dram["x1"]
    from contextlib import ExitStack
    ctx = ExitStack()
    mk.cur_ctx = ctx
    with ctx:
        xt = [mk.sb(f"A_xt{j}_{L}", (128, D), F32) for j in (0, 1)]
        xnt = [mk.sb(f"A_xnt{j}_{L}", (128, D), F32) for j in (0, 1)]

        stage = [mk.sb(f"A_stg{j}_{L}", (128, D), F32R) for j in (0, 1)]
        stats = mk.sb(f"A_stats_{L}", (128, 48), F32)
        k.op("vector", "memset", stats[:], 0.0, inc=k.s("VE"))
        scrV = mk.sb(f"A_scrV_{L}", (128, D), F32)
        scrS = mk.sb(f"A_scrS_{L}", (128, D), F32)
        rsc = mk.sb(f"A_rsc_{L}", (128, 112), F32)
        k.op("vector", "memset", rsc[:], 0.0, inc=k.s("VE"))
        mk._rsc_tmp = rsc
        rscR = mk.sb(f"A_rscR_{L}", (128, 96), F32R)
        h_scat = [0, 0]
        iotn = max(prep.get("iot", 0), prep.get("rbc", 0))
        h_evac = [iotn, iotn]
        h_route = [0, 0]
        h_store = [0, 0]
        h_xn = [0, 0]
        h_sqrd = [0]
        h_tp = [0, 0]
        h_prb = 0
        h_rankmm = 0
        h_sq = [0, 0]
        h_rpsum = [0, 0]
        h_rpsum_sc = [0, 0]
        h_rk = [0, 0]
        n_xnt_dma = [0, 0]
        for i in range(NT):
            p = i % 2
            so = (i % 2) * 24
            r0 = i * 128
            nx = k.op("sync", "dma_start", out=xt[p][:], in_=xsrc[r0:r0 + 128, :],
                      waits=[(VE, h_xn[p]), (SC, h_sq[p])], inc=LD)
            nsq0 = k.op("scalar", "activation", out=scrS[:], in_=xt[p][:],
                        func=AF.Square, waits=[(LD, nx), (VE, h_sqrd[0])], inc=SC)
            for q_ in range(4):
                sl = slice(q_ * 256, (q_ + 1) * 256)
                nsum = k.op("vector", "tensor_reduce", out=stats[:, so + 8 + q_:so + 9 + q_],
                            in_=xt[p][:, sl], op=ALU.add, axis=AXX,
                            waits=[(LD, nx)] if q_ == 0 else (), inc=VE)
                nsq = k.op("vector", "tensor_reduce", out=stats[:, so + 12 + q_:so + 13 + q_],
                            in_=scrS[:, sl], op=ALU.add, axis=AXX,
                            waits=[(SC, nsq0)] if q_ == 0 else (), inc=VE)
            h_sqrd[0] = nsq
            k.wait("vector", VE, VE.n)
            k.op("vector", "tensor_tensor", out=stats[:, so + 16:so + 18],
                 in0=stats[:, so + 8:so + 10], in1=stats[:, so + 10:so + 12], op=ALU.add, inc=VE)
            nlv = k.op("vector", "tensor_tensor", out=stats[:, so + 18:so + 20],
                 in0=stats[:, so + 12:so + 14], in1=stats[:, so + 14:so + 16], op=ALU.add,
                 inc=VE)
            k.wait("vector", VE, nlv)
            nsum = k.op("vector", "tensor_tensor", out=stats[:, so:so + 1],
                        in0=stats[:, so + 16:so + 17], in1=stats[:, so + 17:so + 18], op=ALU.add, inc=VE)
            nsq = k.op("vector", "tensor_tensor", out=stats[:, so + 1:so + 2],
                       in0=stats[:, so + 18:so + 19], in1=stats[:, so + 19:so + 20], op=ALU.add, inc=VE)
            k.wait("vector", VE, nsq)
            nmu = k.op("vector", "tensor_scalar", out=stats[:, so + 2:so + 3],
                       in0=stats[:, so:so + 1], scalar1=1.0 / D, scalar2=None, op0=ALU.mult,
                       inc=VE)
            k.wait("vector", VE, nmu)
            nmu2 = k.op("vector", "tensor_tensor", out=stats[:, so + 3:so + 4],
                        in0=stats[:, so + 2:so + 3], in1=stats[:, so + 2:so + 3], op=ALU.mult,
                        inc=VE)
            k.wait("vector", VE, nmu2)
            nvar = k.op("vector", "scalar_tensor_tensor", out=stats[:, so + 4:so + 5],
                        in0=stats[:, so + 1:so + 2], scalar=1.0 / D, in1=stats[:, so + 3:so + 4],
                        op0=ALU.mult, op1=ALU.subtract, inc=VE)
            nln = k.op("scalar", "activation", out=stats[:, so + 5:so + 6], in_=stats[:, so + 4:so + 5],
                       func=AF.Ln, bias=fx["consts"][:, 0:1], waits=[(VE, nvar), (LD, mk.fixed_loaded)],
                       inc=SC)
            k.wait("scalar", SC, nln)
            nrstd = k.op("scalar", "activation", out=stats[:, so + 6:so + 7], in_=stats[:, so + 5:so + 6],
                         func=AF.Exp, scale=-0.5, inc=SC)
            xnw = [(SC, nrstd), (MM, h_tp[p])]
            if L == 0:
                xnw.append((LD, n_xnt_dma[p]))
            nxn = k.op("vector", "tensor_scalar", out=xnt[p][:], in0=xt[p][:],
                       scalar1=stats[:, so + 2:so + 3], scalar2=stats[:, so + 6:so + 7],
                       op0=ALU.subtract, op1=ALU.mult, waits=xnw, lowp=True, inc=VE)
            h_xn[p] = nxn
            h_sq[p] = nsq0

            if L == 0:
                n_xnt_dma[p] = k.op("sync", "dma_start", out=dram["xn_tok"][r0:r0 + 128, :],
                                    in_=xnt[p][:], waits=[(VE, nxn)], inc=LD)
            ntp = 0
            for c in range(ND):
                bank = 2 * p + c // 4
                off = bank * 512 + (c % 4) * 128
                ntp = k.op("tensor", "transpose", PS[:, off:off + 128],
                           xnt[p][:, c * 128:(c + 1) * 128], fx["ident"][:].bitcast(F32),
                           waits=[(VE, nxn), (SC, h_evac[p]), (LD, mk.fixed_loaded)], inc=MM)
            h_tp[p] = ntp
            nev = 0
            for c in range(ND):
                bank = 2 * p + c // 4
                off = bank * 512 + (c % 4) * 128
                wmm = ntp if c >= 4 else ntp - 4
                nev = k.op("scalar", "activation", out=stage[p][:, c * 128:(c + 1) * 128],
                           in_=PS[:, off:off + 128], func=AF.Copy,
                           waits=[(MM, max(wmm, h_route[p])), (LD, h_store[p])], inc=SC)
            h_evac[p] = nev
            nst = 0
            for c in range(ND):
                nst = k.op("sync", "dma_start",
                           out=dram["xnT"][c * 128:(c + 1) * 128, r0:r0 + 128],
                           in_=stage[p][:, c * 128:(c + 1) * 128],
                           waits=[(SC, nev)] if c == 0 else (), inc=LD)
            h_store[p] = nst
            if L == 0:
                rb = 4 + p
                rz = rb * 512
                nmmr = 0
                for c in range(ND):
                    nmmr = k.op("tensor", "matmul", PS[:, rz:rz + E0],
                                stage[p][:, c * 128:(c + 1) * 128],
                                fx["l0rw"][:, c * E0:(c + 1) * E0],
                                start=(c == 0), stop=(c == ND - 1),
                                waits=[(SC, nev), (VE, h_rpsum[p]), (SC, h_rpsum_sc[p])] if c == 0 else [(SC, nev)],
                                inc=MM)
                h_route[p] = nmmr
                if i >= NT - 2:
                    k.wait("vector", VE, VE.n)
                    k.drain_pad("vector", scrV[:, 0:512])
                nlg = k.op("vector", "tensor_tensor", out=rsc[:, 56:64], in0=PS[:, rz:rz + E0],
                           in1=fx["RBC"][:], op=ALU.add,
                           waits=[(MM, nmmr), (SC, prep["rbc"]), (SC, h_prb)], inc=VE)
                h_rpsum[p] = nlg
                h_rpsum_sc[p] = 0
                k.wait("vector", VE, nlg)
                nm8 = k.op("vector", "max", out=rsc[:, 0:8], in_=rsc[:, 56:64], inc=VE)
                k.wait("vector", VE, nm8)
                nneg = k.op("vector", "tensor_scalar", out=rsc[:, 8:9], in0=rsc[:, 0:1],
                            scalar1=-1.0, scalar2=None, op0=ALU.mult, inc=VE)
                nprb = k.op("scalar", "activation", out=rsc[:, 16:24], in_=rsc[:, 56:64],
                            func=AF.Exp, bias=rsc[:, 8:9], waits=[(VE, nneg)], inc=SC)
                h_prb = nprb
                nmsk = k.op("vector", "tensor_scalar", out=rsc[:, 24:32], in0=rsc[:, 56:64],
                            scalar1=rsc[:, 1:2], scalar2=None, op0=ALU.is_ge,
                            waits=[(SC, nprb)], inc=VE)
                k.wait("vector", VE, nmsk)
                nwu = k.op("vector", "tensor_tensor", out=rsc[:, 32:40], in0=rsc[:, 16:24],
                           in1=rsc[:, 24:32], op=ALU.mult, inc=VE)
                nls = k.op("vector", "tensor_reduce", out=rsc[:, 9:10], in_=rsc[:, 16:24],
                           op=ALU.add, axis=AXX, waits=[(SC, nprb)], inc=VE)
                k.wait("vector", VE, nls)
                nlr = k.op("vector", "reciprocal", rsc[:, 10:11], rsc[:, 9:10], inc=VE)
                k.wait("vector", VE, nlr)
                k.op("vector", "tensor_scalar", out=fx["W0"][:, i * E0:(i + 1) * E0],
                     in0=rsc[:, 32:40], scalar1=rsc[:, 10:11], scalar2=None,
                     op0=ALU.mult, inc=VE)
            else:
                for g in (0, 1):
                    rb = 4 + p
                    rz = rb * 512 + g * 64
                    nmmr = 0
                    for c in range(ND):
                        nmmr = k.op("tensor", "matmul", PS[:, rz:rz + E1],
                                    stage[p][:, c * 128:(c + 1) * 128],
                                    fx["l1rw"][:, g * 256 + c * E1:g * 256 + (c + 1) * E1],
                                    start=(c == 0), stop=(c == ND - 1),
                                    waits=[(SC, nev), (VE, h_rpsum[p]), (SC, h_rpsum_sc[p]),
                                           (LD, prep[f"rw{g}"])] if c == 0 else [(SC, nev)],
                                    inc=MM)
                    h_route[p] = nmmr
                    if i >= NT - 2:
                        k.wait("vector", VE, VE.n)
                        k.drain_pad("vector", scrV[:, 0:512])
                    nm8 = k.op("vector", "max", out=rsc[:, 0:8], in_=PS[:, rz:rz + E1],
                               waits=[(MM, nmmr)], inc=VE)
                    k.wait("vector", VE, nm8)
                    nneg = k.op("vector", "tensor_scalar", out=rsc[:, 8:9], in0=rsc[:, 0:1],
                                scalar1=-1.0, scalar2=None, op0=ALU.mult,
                                waits=[(SC, h_prb)], inc=VE)
                    nprb = k.op("scalar", "activation", out=rsc[:, 16:48], in_=PS[:, rz:rz + E1],
                                func=AF.Exp, bias=rsc[:, 8:9], waits=[(VE, nneg)], inc=SC)
                    h_prb = nprb
                    nmsk = k.op("vector", "tensor_scalar", out=rsc[:, 16 + 32:16 + 64], in0=PS[:, rz:rz + E1],
                                scalar1=rsc[:, 1:2], scalar2=None, op0=ALU.is_ge,
                                waits=[(SC, nprb)], inc=VE)
                    h_rpsum[p] = nmsk
                    h_rpsum_sc[p] = nprb
                    nls = k.op("vector", "tensor_reduce", out=rsc[:, 9:10], in_=rsc[:, 16:48],
                               op=ALU.add, axis=AXX, inc=VE)
                    k.wait("vector", VE, nls)
                    nwu = k.op("vector", "tensor_tensor", out=rsc[:, 80:112], in0=rsc[:, 16:48],
                               in1=rsc[:, 48:80], op=ALU.mult, inc=VE)
                    nlr = k.op("vector", "reciprocal", rsc[:, 10:11], rsc[:, 9:10], inc=VE)
                    k.wait("vector", VE, nlr)
                    k.op("vector", "tensor_scalar",
                         out=fx[f"W1{g}"][:, i * E1:(i + 1) * E1], in0=rsc[:, 80:112],
                         scalar1=rsc[:, 10:11], scalar2=0.5, op0=ALU.mult, op1=ALU.mult,
                         inc=VE)
        k.op("tensor", "matmul", PS[0:1, 0:2], fx["ones_row"][0:1, 0:1],
             fx["ones_row"][0:1, 0:2], start=True, stop=True,
             waits=[(SC, h_evac[0]), (SC, h_evac[1]), (VE, h_rpsum[0]),
                    (VE, h_rpsum[1])], inc=MM)
        if mk.dbg:
            ddr = mk.dout(f"dbgRSC{L}", (128, 112), F32)
            k.op("sync", "dma_start", out=ddr, in_=mk._rsc_tmp[:],
                 waits=[(k.s("VE"), k.s("VE").n), (k.s("SC"), k.s("SC").n)], inc=k.s("LD"))
            dst = mk.dout(f"dbgSTATS{L}", (128, 48), F32)
            k.op("sync", "dma_start", out=dst, in_=stats[:], inc=k.s("LD"))
        k.barrier()


def build(layers=(0, 1), phases=("A", "B0", "B", "B2", "C", "C1", "D"),
          dbg=False, dbg_in=(), nodetect=True):
    from contextlib import ExitStack
    nc = bass.Bass("TRN2", target_bir_lowering=False, detect_race_conditions=not nodetect)
    with ExitStack() as ctx:
        k = K(nc, ctx)
        mk = MK(nc, ctx, k, dbg=dbg)
        mk.dbg_in = set(dbg_in)
        mk.declare_dram()
        mk.PS = ctx.enter_context(nc.psum_tensor("PS", [128, 4096], F32))
        mk.cur_ctx = None
        mk.load_fixed()
        for L in layers:
            prep = layer_prep(mk, L)
            if "A" in phases:
                phase_A(mk, L, prep)
                mk.cur_ctx = None
            if "B0" in phases:
                phase_B0(mk, L, prep)
                mk.cur_ctx = None
            if "B" in phases:
                phase_B(mk, L)
                mk.cur_ctx = None
            if "B2" in phases and L == 0:
                phase_B2(mk)
                mk.cur_ctx = None
            if "C" in phases and L == 0:
                phase_C(mk)
                mk.cur_ctx = None
            if "C1" in phases and L == 1:
                phase_C1(mk)
                mk.cur_ctx = None
            if "D" in phases:
                phase_D(mk, L)
                mk.cur_ctx = None
        if dbg:
            fx = mk.fixed
            dd = mk.dout("dbgW0", (128, NT * E0), F32)

            dr = mk.dout("dbgRT", (128, 2 * NT * 4), F32)
            dsl = mk.dout("dbgSLOT", (128, 2 * NT * 2), U32)
            k.op("sync", "dma_start", out=dd, in_=fx["W0"][:], inc=k.s("LD"))
            k.op("sync", "dma_start", out=dr, in_=fx["RT"][:], inc=k.s("LD"))
            k.op("sync", "dma_start", out=dsl, in_=fx["SLOT"][:], inc=k.s("LD"))
        k.barrier()
        with nc.sbuf_tensor("final_flush", [128, D], F32) as ffl:
            k.op("gpsimd", "dma_start", out=ffl[:], in_=mk.dram["gacc"][T - 128:T, :],
                 waits=[(k.s("GP"), k.s("GP").n)], inc=k.s("GP"))
        k.wait("sync", k.s("LD"), k.s("LD").n)
        k.wait("sync", k.s("GP"), k.s("GP").n)
        k.wait("sync", k.s("SQ"), k.s("SQ").n)
        block = ctx.enter_context(nc.Block())
        k.finalize(block)
    return nc





def phase_B0(mk, L, prep=None):
    k, nc, fx, dram = mk.k, mk.nc, mk.fixed, mk.dram
    LD, GP, MM, VE, SC = (k.s(x) for x in ("LD", "GP", "MM", "VE", "SC"))
    PS = mk.PS
    from contextlib import ExitStack
    ctx = ExitStack()
    mk.cur_ctx = ctx
    with ctx:
        xnTh = mk.sb(f"B0_xnTh_{L}", (128, ND * 2048), F32R)
        wbuf = mk.sb(f"B0_wbuf_{L}", (128, 8192), F32R)
        latsb = mk.sb(f"B0_lat_{L}", (128, 2 * 2048), F32R) if L == 1 else None
        stg = [mk.sb(f"B0_stg{j}_{L}", (128, 1024), F32R) for j in (0, 1)]
        vstg = [mk.sb(f"B0_vstg{j}_{L}", (128, 1040), F32R) for j in (0, 1)]
        brow = mk.sb(f"B0_brow_{L}", (1, 1024), F32R)
        onec = mk.sb(f"B0_onec_{L}", (128, 1), F32)

        nz = k.op("vector", "memset", onec[:], 1.0, inc=VE)
        nvo = 0
        for j in (0, 1):
            nvo = k.op("vector", "tensor_copy",
                       vstg[j][:].rearrange("p (h q) -> p h q", h=16)[:, :, 64:65],
                       onec[:].broadcast_to([128, 16, 1]),
                       waits=[(VE, nz)], lowp=True, inc=VE)
        nbv = k.op("sync", "dma_start", out=brow[:],
                   in_=dram["l0_bv_row" if L == 0 else "l1_bv_row"], inc=LD)

        h_bank_sc = [0, 0, 0, 0]
        h_vbank = [0, 0]
        h_stg = [0, 0]
        h_vstg = [nvo, nvo]
        h_vstg_ld = [0, 0]
        h_wbuf = 0

        def proj_pass(which, half, nxn_ld, nw_ld):
            nonlocal h_wbuf
            tok0 = half * 2048
            res = []
            if which in ("q", "k0"):
                outsT, nk, bias_col, scale = (dram["qT"], ND, fx["bcol"][:, 32:40], None) \
                    if which == "q" else (dram["kT"], ND, fx["bcol"][:, 8:16], None)
                for o in range(ND):
                    for n in range(4):
                        bank = (o * 4 + n) % 4
                        zone = PS[:, bank * 512:(bank + 1) * 512]
                        nm = 0
                        for c in range(ND):
                            w0 = []
                            if c == 0:
                                w0 = [(SC, h_bank_sc[bank]), (MM, 0)]
                                w0.append((LD, max(nxn_ld, nw_ld)))
                            nm = k.op("tensor", "matmul", zone,
                                      wbuf[:, c * 1024 + o * 128:c * 1024 + (o + 1) * 128],
                                      xnTh[:, c * 2048 + n * 512:c * 2048 + (n + 1) * 512],
                                      start=(c == 0), stop=(c == ND - 1), waits=w0, inc=MM)
                        j = n % 2
                        if which == "q":
                            sc_kw = dict(scale=fx["bcol"][:, 24 + o:25 + o]) if L == 0 else dict(scale=SCALE)
                        else:
                            sc_kw = {}
                        wq_ = [(MM, nm), (LD, h_stg[j])]
                        if prep and "qs" in prep:
                            wq_.append((k.s("VE"), prep["qs"]))
                        nev = k.op("scalar", "activation", out=stg[j][:, 0:512],
                                   in_=zone, func=AF.Identity, bias=bias_col[:, o:o + 1],
                                   waits=wq_, inc=SC, **sc_kw)
                        h_bank_sc[bank] = nev
                        h_stg[j] = k.op("sync", "dma_start",
                                        out=outsT[o * 128:(o + 1) * 128, tok0 + n * 512:tok0 + (n + 1) * 512],
                                        in_=stg[j][:, 0:512], waits=[(SC, nev)], inc=LD)
            elif which == "lat":
                for o in range(2):
                    for n in range(4):
                        bank = (o * 4 + n) % 4
                        zone = PS[:, bank * 512:(bank + 1) * 512]
                        nm = 0
                        for c in range(ND):
                            w0 = [(SC, h_bank_sc[bank]), (LD, max(nxn_ld, nw_ld))] if c == 0 else []
                            nm = k.op("tensor", "matmul", zone,
                                      wbuf[:, c * 256 + o * 128:c * 256 + (o + 1) * 128],
                                      xnTh[:, c * 2048 + n * 512:c * 2048 + (n + 1) * 512],
                                      start=(c == 0), stop=(c == ND - 1), waits=w0, inc=MM)
                        nev = k.op("scalar", "activation", out=latsb[:, o * 2048 + n * 512:o * 2048 + (n + 1) * 512],
                                   in_=zone, func=AF.Identity, bias=fx["bcol"][:, 16 + o:17 + o],
                                   waits=[(MM, nm)], inc=SC)
                        h_bank_sc[bank] = nev
                        res.append(nev)
            elif which == "k1":
                for o in range(ND):
                    for n in range(4):
                        bank = (o * 4 + n) % 4
                        zone = PS[:, bank * 512:(bank + 1) * 512]
                        nm = 0
                        for c in range(2):
                            w0 = [(SC, h_bank_sc[bank]), (LD, nw_ld), (SC, nxn_ld)] if c == 0 else []
                            nm = k.op("tensor", "matmul", zone,
                                      wbuf[:, c * 1024 + o * 128:c * 1024 + (o + 1) * 128],
                                      latsb[:, c * 2048 + n * 512:c * 2048 + (n + 1) * 512],
                                      start=(c == 0), stop=(c == 1), waits=w0, inc=MM)
                        j = n % 2
                        nev = k.op("scalar", "activation", out=stg[j][:, 0:512],
                                   in_=zone, func=AF.Identity, bias=fx["bcol"][:, 8 + o:9 + o],
                                   waits=[(MM, nm), (LD, h_stg[j])], inc=SC)
                        h_bank_sc[bank] = nev
                        h_stg[j] = k.op("sync", "dma_start",
                                        out=dram["kT"][o * 128:(o + 1) * 128, tok0 + n * 512:tok0 + (n + 1) * 512],
                                        in_=stg[j][:, 0:512], waits=[(SC, nev)], inc=LD)
            elif which in ("v0", "v1"):
                nlat = nxn_ld
                for tt in range(16):
                    z = tt % 2
                    zone = PS[:, (4 + 2 * z) * 512:(6 + 2 * z) * 512]
                    nm = 0
                    ncon = ND if which == "v0" else 2
                    lhs_src = xnTh if which == "v0" else latsb
                    for fh in range(2):
                        for c in range(ncon):
                            w0 = []
                            if fh == 0 and c == 0:
                                w0 = [(VE, h_vbank[z]), (LD, max(nw_ld, nbv))]
                                if which == "v0":
                                    w0.append((LD, nxn_ld))
                                else:
                                    w0.append((SC, nlat))
                            nm = k.op("tensor", "matmul",
                                      zone[:, fh * 512:(fh + 1) * 512],
                                      lhs_src[:, c * 2048 + tt * 128:c * 2048 + (tt + 1) * 128],
                                      wbuf[:, c * 1024 + fh * 512:c * 1024 + (fh + 1) * 512],
                                      start=(c == 0), stop=False, waits=w0, inc=MM)
                        nm = k.op("tensor", "matmul", zone[:, fh * 512:(fh + 1) * 512],
                                  fx["ones_row"][:], brow[0:1, fh * 512:(fh + 1) * 512],
                                  start=False, stop=True, inc=MM)
                    j = tt % 2
                    nev = 0
                    for fh in range(2):
                        nev = k.op("vector", "tensor_copy",
                                   vstg[j][:].rearrange("p (h q) -> p h q", h=16)[:, fh * 8:(fh + 1) * 8, 0:64],
                                   zone[:, fh * 512:(fh + 1) * 512].rearrange("p (h q) -> p h q", h=8),
                                   waits=[(MM, nm), (LD, h_vstg_ld[j])] if fh == 0 else (),
                                   lowp=True, inc=VE)
                    h_vbank[z] = nev
                    h_vstg_ld[j] = k.op("sync", "dma_start",
                                        out=dram["vtok"][tok0 + tt * 128:tok0 + (tt + 1) * 128, :],
                                        in_=vstg[j][:], waits=[(VE, nev)], inc=LD)

        for half in (0, 1):
            tok0 = half * 2048
            nxn = 0
            for c in range(ND):
                nxn = k.op("sync", "dma_start", out=xnTh[:, c * 2048:(c + 1) * 2048],
                           in_=dram["xnT"][c * 128:(c + 1) * 128, tok0:tok0 + 2048],
                           waits=[(MM, h_wbuf)] if c == 0 else (), inc=LD)
            if L == 0:
                for wname, which in (("l0_wqT", "q"), ("l0_wkT", "k0"), ("l0_wvT", "v0")):
                    nw = k.op("sync", "dma_start",
                              out=wbuf[:].rearrange("p (c f) -> p c f", c=ND),
                              in_=dram[wname].rearrange("(c p) f -> p c f", p=128),
                              waits=[(MM, h_wbuf)], inc=LD)
                    proj_pass(which, half, nxn, nw)
                    h_wbuf = MM.n
            else:
                nw = k.op("sync", "dma_start",
                          out=wbuf[:].rearrange("p (c f) -> p c f", c=ND),
                          in_=dram["l1_wq"].rearrange("(c p) f -> p c f", p=128),
                          waits=[(MM, h_wbuf)], inc=LD)
                proj_pass("q", half, nxn, nw)
                h_wbuf = MM.n
                nw = k.op("sync", "dma_start",
                          out=wbuf[:, 0:2048].rearrange("p (c f) -> p c f", c=ND),
                          in_=dram["l1_wl"].rearrange("(c p) f -> p c f", p=128),
                          waits=[(MM, h_wbuf)], inc=LD)
                proj_pass("lat", half, nxn, nw)
                h_wbuf = MM.n
                nlat_done = SC.n
                nw = k.op("sync", "dma_start",
                          out=wbuf[:, 0:2048].rearrange("p (c f) -> p c f", c=2),
                          in_=dram["l1_wk"].rearrange("(c p) f -> p c f", p=128),
                          waits=[(MM, h_wbuf)], inc=LD)
                proj_pass("k1", half, nlat_done, nw)
                h_wbuf = MM.n
                nw = k.op("sync", "dma_start",
                          out=wbuf[:, 0:2048].rearrange("p (c f) -> p c f", c=2),
                          in_=dram["l1_wv"].rearrange("(c p) f -> p c f", p=128),
                          waits=[(MM, h_wbuf)], inc=LD)
                proj_pass("v1", half, nlat_done, nw)
                h_wbuf = MM.n
        k.barrier()
    mk.cur_ctx = None



def phase_B(mk, L):
    k, nc, fx, dram = mk.k, mk.nc, mk.fixed, mk.dram
    LD, GP, MM, VE, SC = (k.s(x) for x in ("LD", "GP", "MM", "VE", "SC"))
    PS = mk.PS
    from contextlib import ExitStack
    ctx = ExitStack()
    mk.cur_ctx = ctx
    with ctx:
        qp = [mk.sb(f"B_qp{j}_{L}", (128, T), F32R) for j in (0, 1)]
        kp = [mk.sb(f"B_kp{j}_{L}", (128, T), F32R) for j in (0, 1)]
        vp = [mk.sb(f"B_vp{j}_{L}", (128, 32 * 130), F32R) for j in (0, 1)]
        wob = [mk.sb(f"B_wob{j}_{L}", (128, 1024), F32R) for j in (0, 1)]
        sqb = mk.sb(f"B_sqb_{L}", (128, T), F32R)
        pb = [mk.sb(f"B_pb{j}_{L}", (128, 1024), F32R) for j in (0, 1, 2)]
        ao = mk.sb(f"B_ao_{L}", (128, T), F32R)
        binv = mk.sb(f"B_binv_{L}", (64, 512), F32)
        lrec = mk.sb(f"B_lrec_{L}", (1, 512), F32R)
        nrw = mk.sb(f"B_nrw_{L}", (1, 16), F32)
        cv = mk.sb(f"B_cv_{L}", (1, 4), F32R)
        csc = mk.sb(f"B_csc_{L}", (128, 2), F32)
        borow = mk.sb(f"B_borow_{L}", (1, 1024), F32R)
        stgB = [mk.sb(f"B_stgB{j}_{L}", (128, 1024), F32) for j in (0, 1)]

        nbo = k.op("sync", "dma_start", out=borow[:],
                   in_=dram["l0_bo_row" if L == 0 else "l1_bo_row"], inc=LD)
        woT = dram["l0_woT"] if L == 0 else dram["l1_wo"]

        h_pair_rd = [0, 0]
        h_sacc = [0] * NT
        h_wob_rd = [0, 0]
        h_stgB = [0, 0]
        n_loads = [0, 0]
        n_wob = [0, 0]
        h_sq_mm = 0
        h_pv_cons = 0
        h_bc_cons = 0
        h_sc_banks = [0] * 6
        h_pb_rd = [0, 0, 0]
        h_lrec_rd = 0

        for p in range(NP):
            j = p % 2
            n1 = k.op("sync", "dma_start", out=qp[j][:], in_=dram["qT"][p * 128:(p + 1) * 128, :],
                      waits=[(MM, h_pair_rd[j])], inc=LD)
            n2 = k.op("sync", "dma_start", out=kp[j][:], in_=dram["kT"][p * 128:(p + 1) * 128, :],
                      inc=LD)
            n3 = k.op("sync", "dma_start",
                      out=vp[j][:].rearrange("q (c u) -> q c u", c=32),
                      in_=dram["vtok"][:, p * 130:(p + 1) * 130].rearrange("(c q) u -> q c u", q=128),
                      inc=LD)
            n_loads[j] = n3
            n_wob[j] = k.op("sync", "dma_start", out=wob[j][:], in_=woT[p * 128:(p + 1) * 128, :],
                            waits=[(MM, h_wob_rd[j])], inc=LD)
            for hl in range(2):
                nsq = k.op("scalar", "activation", out=sqb[:], in_=kp[j][:].bitcast(F32) if hl == 0 else qp[j][:].bitcast(F32),
                           func=AF.Square, waits=[(LD, n_loads[j]), (MM, h_sq_mm)], inc=SC)
                for hh in range(2):
                    hr0 = hh * 64
                    nmn = 0
                    for n in range(8):
                        nmn = k.op("tensor", "matmul", PS[0:1, n * 512:n * 512 + 512],
                                   fx["ones_col"][hr0:hr0 + 64, :],
                                   sqb[hr0:hr0 + 64, n * 512:(n + 1) * 512],
                                   start=True, stop=True,
                                   waits=[(SC, nsq), (VE, VE.n)] if n == 0 else (), inc=MM)
                    k.op("vector", "tensor_reduce", out=nrw[0:1, hl * 4 + hh:hl * 4 + hh + 1],
                         in_=PS[0:1, 0:4096], op=ALU.max, axis=AXX,
                         waits=[(MM, nmn)], inc=VE)
                    k.wait("tensor", VE, VE.n)
                h_sq_mm = MM.n
            nprod = 0
            for hh in range(2):
                nprod = k.op("vector", "tensor_tensor", out=nrw[0:1, 8 + hh:9 + hh],
                             in0=nrw[0:1, hh:hh + 1], in1=nrw[0:1, 4 + hh:5 + hh],
                             op=ALU.mult, waits=[(VE, VE.n)] if hh == 0 else (), inc=VE)
            nln = k.op("scalar", "activation", out=nrw[0:1, 10:12], in_=nrw[0:1, 8:10],
                       func=AF.Ln, bias=fx["consts"][0:1, 1:2], waits=[(VE, nprod)], inc=SC)
            k.wait("scalar", SC, nln)
            ncv = k.op("scalar", "activation", out=cv[:, 0:2], in_=nrw[0:1, 10:12],
                       func=AF.Exp, scale=0.5, inc=SC)
            ncsc = 0
            for hh in range(2):
                nbc = k.op("tensor", "matmul", PS[:, 3072 + hh * 2: 3072 + hh * 2 + 2],
                           fx["ones_row"][:], cv[0:1, 0:2],
                           start=True, stop=True,
                           waits=[(SC, ncv), (SC, h_bc_cons)] if hh == 0 else (), inc=MM)
                ncsc = k.op("scalar", "activation", out=csc[:, hh:hh + 1],
                            in_=PS[:, 3072 + hh * 2 + hh:3072 + hh * 2 + hh + 1],
                            func=AF.Copy, scale=-1.0, waits=[(MM, nbc)], inc=SC)
            for hh in range(2):
                hr0 = hh * 64
                for n in range(8):
                    q0 = n * 512
                    npv = 0
                    for grp in range(16):
                        b2 = (grp % 3) * 2
                        nsc2 = 0
                        for s2 in range(2):
                            kc = grp * 2 + s2
                            w0 = [(SC, h_sc_banks[b2 + s2])]
                            if grp == 0 and s2 == 0:
                                w0 += [(VE, h_pv_cons), (LD, n_loads[j]), (SC, ncsc)]
                            nsc2 = k.op("tensor", "matmul",
                                        PS[:, (b2 + s2) * 512:(b2 + s2 + 1) * 512],
                                        kp[j][hr0:hr0 + 64, kc * 128:(kc + 1) * 128],
                                        qp[j][hr0:hr0 + 64, q0:q0 + 512],
                                        start=True, stop=True, waits=w0, inc=MM)
                        slot = grp % 3
                        nexp = k.op("scalar", "activation", out=pb[slot][:],
                                    in_=PS[:, b2 * 512:(b2 + 2) * 512], func=AF.Exp,
                                    bias=csc[:, hh:hh + 1],
                                    waits=[(MM, nsc2), (MM, h_pb_rd[slot])], inc=SC)
                        h_sc_banks[b2] = nexp
                        h_sc_banks[b2 + 1] = nexp
                        for s2 in range(2):
                            kc = grp * 2 + s2
                            npv = k.op("tensor", "matmul", PS[0:65, 3584:4096],
                                       vp[j][:, kc * 130 + hh * 65:kc * 130 + (hh + 1) * 65],
                                       pb[slot][:, s2 * 512:(s2 + 1) * 512],
                                       start=(grp == 0 and s2 == 0), stop=(grp == 15 and s2 == 1),
                                       waits=[(SC, nexp)] if s2 == 0 else (), inc=MM)
                        h_pb_rd[slot] = npv
                    nlr = k.op("vector", "reciprocal", lrec[:], PS[64:65, 3584:4096],
                               waits=[(MM, max(npv, h_lrec_rd))], lowp=True, inc=VE)
                    nbc = k.op("tensor", "matmul", PS[0:64, 3072:3584],
                               fx["ones_row"][0:1, 0:64], lrec[:],
                               start=True, stop=True, waits=[(VE, nlr), (SC, h_bc_cons)], inc=MM)
                    h_lrec_rd = nbc
                    nbi = k.op("scalar", "activation", out=binv[:], in_=PS[0:64, 3072:3584],
                               func=AF.Copy, waits=[(MM, nbc)], inc=SC)
                    h_bc_cons = nbi
                    h_pv_cons = k.op("vector", "tensor_tensor", out=ao[hr0:hr0 + 64, q0:q0 + 512],
                                     in0=PS[0:64, 3584:4096], in1=binv[:], op=ALU.mult,
                                     waits=[(SC, nbi)], lowp=True, inc=VE)
            for tt in range(NT):
                z = tt % 2
                zone = PS[:, z * 1024:(z + 1) * 1024]
                nm = 0
                for fh in range(2):
                    w0 = []
                    if fh == 0:
                        w0 = [(VE, h_pv_cons), (LD, n_wob[j]), (SC, h_sc_banks[z * 2]), (SC, h_sc_banks[z * 2 + 1])]
                    nm = k.op("tensor", "matmul", zone[:, fh * 512:(fh + 1) * 512],
                              ao[:, tt * 128:(tt + 1) * 128],
                              wob[j][:, fh * 512:(fh + 1) * 512],
                              start=True, stop=(p != 0), waits=w0, inc=MM)
                    if p == 0:
                        nm = k.op("tensor", "matmul", zone[:, fh * 512:(fh + 1) * 512],
                                  fx["ones_row"][:], borow[0:1, fh * 512:(fh + 1) * 512],
                                  start=False, stop=True,
                                  waits=[(LD, nbo)] if fh == 0 and tt == 0 else (), inc=MM)
                nev = k.op("scalar", "activation", out=stgB[z][:], in_=zone, func=AF.Copy,
                           waits=[(MM, nm), (GP, h_stgB[z])], inc=SC)
                h_sc_banks[z * 2] = nev
                h_sc_banks[z * 2 + 1] = nev
                kw = {} if p == 0 else dict(accum_op=ALU.add)
                h_stgB[z] = k.op("gpsimd", "dma_start", out=dram["sacc"][tt * 128:(tt + 1) * 128, :],
                                 in_=stgB[z][:], waits=[(SC, nev), (GP, h_sacc[tt])], inc=GP, **kw)
                h_sacc[tt] = h_stgB[z]
            h_pair_rd[j] = MM.n
            h_wob_rd[j] = MM.n
        k.barrier()
    mk.cur_ctx = None


def emit_ln_tile(mk, src_ap, stats, so, scrS, waits_in, eps_col):
    k = mk.k
    VE, SC = k.s("VE"), k.s("SC")
    ALUa = ALU.add
    nsq0 = k.op("scalar", "activation", out=scrS[:], in_=src_ap, func=AF.Square,
                waits=list(waits_in) + [(VE, mk._ln_scr_rd)], inc=SC)
    nred = 0
    for q_ in range(4):
        sl = slice(q_ * 256, (q_ + 1) * 256)
        k.op("vector", "tensor_reduce", out=stats[:, so + 8 + q_:so + 9 + q_],
             in_=src_ap[:, sl] if hasattr(src_ap, "shape") else src_ap,
             op=ALUa, axis=AXX, waits=list(waits_in) if q_ == 0 else (), inc=VE)
        nred = k.op("vector", "tensor_reduce", out=stats[:, so + 12 + q_:so + 13 + q_],
                    in_=scrS[:, sl], op=ALUa, axis=AXX,
                    waits=[(SC, nsq0)] if q_ == 0 else (), inc=VE)
    mk._ln_scr_rd = nred
    k.wait("vector", VE, nred)
    k.op("vector", "tensor_tensor", out=stats[:, so + 16:so + 18],
         in0=stats[:, so + 8:so + 10], in1=stats[:, so + 10:so + 12], op=ALUa, inc=VE)
    nl = k.op("vector", "tensor_tensor", out=stats[:, so + 18:so + 20],
              in0=stats[:, so + 12:so + 14], in1=stats[:, so + 14:so + 16], op=ALUa, inc=VE)
    k.wait("vector", VE, nl)
    k.op("vector", "tensor_tensor", out=stats[:, so:so + 1],
         in0=stats[:, so + 16:so + 17], in1=stats[:, so + 17:so + 18], op=ALUa, inc=VE)
    nl = k.op("vector", "tensor_tensor", out=stats[:, so + 1:so + 2],
              in0=stats[:, so + 18:so + 19], in1=stats[:, so + 19:so + 20], op=ALUa, inc=VE)
    k.wait("vector", VE, nl)
    nmu = k.op("vector", "tensor_scalar", out=stats[:, so + 2:so + 3],
               in0=stats[:, so:so + 1], scalar1=1.0 / D, scalar2=None, op0=ALU.mult, inc=VE)
    k.wait("vector", VE, nmu)
    nmu2 = k.op("vector", "tensor_tensor", out=stats[:, so + 3:so + 4],
                in0=stats[:, so + 2:so + 3], in1=stats[:, so + 2:so + 3], op=ALU.mult, inc=VE)
    k.wait("vector", VE, nmu2)
    nvar = k.op("vector", "scalar_tensor_tensor", out=stats[:, so + 4:so + 5],
                in0=stats[:, so + 1:so + 2], scalar=1.0 / D, in1=stats[:, so + 3:so + 4],
                op0=ALU.mult, op1=ALU.subtract, inc=VE)
    nln = k.op("scalar", "activation", out=stats[:, so + 5:so + 6], in_=stats[:, so + 4:so + 5],
               func=AF.Ln, bias=eps_col, waits=[(VE, nvar)], inc=SC)
    k.wait("scalar", SC, nln)
    nrstd = k.op("scalar", "activation", out=stats[:, so + 6:so + 7], in_=stats[:, so + 5:so + 6],
                 func=AF.Exp, scale=-0.5, inc=SC)
    return stats[:, so + 2:so + 3], stats[:, so + 6:so + 7], nrstd



def phase_B2(mk):
    k, nc, fx, dram = mk.k, mk.nc, mk.fixed, mk.dram
    LD, GP, MM, VE, SC = k.s("LD"), k.s("GP"), k.s("MM"), k.s("VE"), k.s("SC")
    SQ = k.s("SQ")
    PS = mk.PS
    from contextlib import ExitStack
    ctx = ExitStack()
    mk.cur_ctx = ctx
    mk._ln_scr_rd = 0
    with ctx:
        sa = [mk.sb(f"B2_sa{j}", (128, D), F32) for j in (0, 1)]
        xnl = [mk.sb(f"B2_xn{j}", (128, D), F32) for j in (0, 1)]
        yt = [mk.sb(f"B2_y{j}", (128, D), F32) for j in (0, 1)]
        st_ = [mk.sb(f"B2_s{j}", (128, D), F32) for j in (0, 1)]
        stats = mk.sb("B2_stats", (128, 48), F32)
        scrS = mk.sb("B2_scrS", (128, D), F32)
        gB = mk.sb("B2_gB", (128, D), F32)
        bB = mk.sb("B2_bB", (128, D), F32)
        grow = mk.sb("B2_grow", (1, 2 * D), F32R)

        ng = k.op("sync", "dma_start", out=grow[0:1, 0:D], in_=dram["l0_gamma_row"], inc=LD)
        ng = k.op("sync", "dma_start", out=grow[0:1, D:2 * D], in_=dram["l0_beta_row"], inc=LD)
        nb = 0
        for which, dst in ((0, gB), (1, bB)):
            for fh in range(2):
                w0 = []
                if fh == 0:
                    w0 = [(LD, ng)] if which == 0 else [(SC, nb)]
                nm = k.op("tensor", "matmul", PS[:, fh * 512:(fh + 1) * 512],
                          fx["ones_row"][:], grow[0:1, which * D + fh * 512:which * D + (fh + 1) * 512],
                          start=True, stop=True, waits=w0, inc=MM)
            nb = k.op("scalar", "activation", out=dst[:], in_=PS[:, 0:1024], func=AF.Copy,
                      waits=[(MM, nm)], inc=SC)
        h_y = [0, 0]
        h_s = [0, 0]
        for i in range(NT):
            p = i % 2
            so = p * 24
            r0 = i * 128
            n1 = k.op("sync", "dma_start", out=sa[p][:], in_=dram["sacc"][r0:r0 + 128, :],
                      waits=[(VE, h_y[p]), (GP, k.s("GP").n)] if i < 2 else [(VE, h_y[p])], inc=LD)
            n2 = k.op("sync", "dma_start", out=xnl[p][:], in_=dram["xn_tok"][r0:r0 + 128, :],
                      inc=LD)
            ny = k.op("vector", "tensor_tensor", out=yt[p][:], in0=sa[p][:], in1=xnl[p][:],
                      op=ALU.add, waits=[(LD, n2), (SQ, h_s[p])], inc=VE)
            h_y[p] = ny
            mu, rstd, nrstd = emit_ln_tile(mk, yt[p][:], stats, so, scrS,
                                           [(VE, ny)], fx["consts"][:, 0:1])
            nsn = k.op("vector", "tensor_scalar", out=scrS[:], in0=yt[p][:],
                       scalar1=mu, scalar2=rstd, op0=ALU.subtract, op1=ALU.mult,
                       waits=[(SC, nrstd)], inc=VE)
            mk._ln_scr_rd = nsn
            k.wait("vector", VE, nsn)
            ngm = k.op("vector", "tensor_tensor", out=yt[p][:], in0=scrS[:], in1=gB[:],
                       op=ALU.mult, waits=[(SC, nb)] if i == 0 else (), inc=VE)
            mk._ln_scr_rd = ngm
            k.wait("vector", VE, ngm)
            nsf = k.op("vector", "tensor_tensor", out=st_[p][:], in0=yt[p][:], in1=bB[:],
                       op=ALU.add, inc=VE)
            h_s[p] = k.op("scalar", "dma_start", out=dram["sfin"][r0:r0 + 128, :],
                          in_=st_[p][:], waits=[(VE, nsf)], inc=SQ, inc_n=16)
        k.barrier()
    mk.cur_ctx = None



def phase_C(mk):
    k, nc, fx, dram = mk.k, mk.nc, mk.fixed, mk.dram
    LD, GP, MM, VE, SC = k.s("LD"), k.s("GP"), k.s("MM"), k.s("VE"), k.s("SC")
    PS = mk.PS
    from contextlib import ExitStack
    ctx = ExitStack()
    mk.cur_ctx = ctx
    with ctx:
        w1b = mk.sb("C_w1", (128, ND * HID0), F32R)
        w2b = mk.sb("C_w2", (128, (HID0 // 128) * D), F32R)
        h1b = mk.sb("C_h1", (128, (HID0 // 128) * 512), F32R)
        xch = mk.sb("C_xch", (128, ND * 512), F32R)
        stg = [mk.sb(f"C_stg{j}", (128, D), F32) for j in (0, 1)]
        bcol = mk.sb("C_bcol", (128, 16), F32)
        b2row = mk.sb("C_b2row", (1, D), F32R)

        h_w_rd = 0
        h_x_rd = 0
        h_h1_rd = 0
        h_stg = [0, 0]
        h_zone_sc = [0, 0]
        h_gacc = [0] * NT
        nh1_sc = 0
        for em in range(-1, E0):
            if em < 0:
                w1src, w2src = dram["l0_sw1"], dram["l0_sw2"]
                b1src, b2src = dram["l0_sb1"], dram["l0_sb2_row"]
            else:
                w1src, w2src = dram["l0_ew1"][em], dram["l0_ew2"][em]
                b1src, b2src = dram["l0_eb1"][em], dram["l0_eb2"][em]
            nw = k.op("sync", "dma_start", out=w1b[:].rearrange("p (c f) -> p c f", c=ND),
                      in_=w1src.rearrange("(c p) f -> p c f", p=128),
                      waits=[(MM, h_w_rd)], inc=LD)
            nw = k.op("sync", "dma_start", out=w2b[:].rearrange("p (c f) -> p c f", c=HID0 // 128),
                      in_=w2src.rearrange("(c p) f -> p c f", p=128), inc=LD)
            nw = k.op("sync", "dma_start", out=bcol[:], in_=b1src, inc=LD)
            nw = k.op("sync", "dma_start", out=b2row[:], in_=b2src, inc=LD)
            for t5 in range(8):
                tok0 = t5 * 512
                nx = 0
                for c in range(ND):
                    nx = k.op("sync", "dma_start", out=xch[:, c * 512:(c + 1) * 512],
                              in_=dram["xnT"][c * 128:(c + 1) * 128, tok0:tok0 + 512],
                              waits=[(MM, h_x_rd)] if c == 0 else (), inc=LD)
                for hc in range(HID0 // 128):
                    bank = hc % 2
                    zone = PS[:, bank * 512:(bank + 1) * 512]
                    nm = 0
                    for c in range(ND):
                        w0 = []
                        if c == 0:
                            w0 = [(SC, nh1_sc), (LD, max(nw, nx)), (MM, 0)]
                        nm = k.op("tensor", "matmul", zone,
                                  w1b[:, c * HID0 + hc * 128:c * HID0 + (hc + 1) * 128],
                                  xch[:, c * 512:(c + 1) * 512],
                                  start=(c == 0), stop=(c == ND - 1), waits=w0, inc=MM)
                    nh1_sc = k.op("scalar", "activation", out=h1b[:, hc * 512:(hc + 1) * 512],
                                  in_=zone, func=AF.Gelu, bias=bcol[:, hc:hc + 1],
                                  waits=[(MM, max(nm, h_h1_rd))], inc=SC)
                h_x_rd = MM.n
                for tt in range(4):
                    z = tt % 2
                    zone = PS[:, (2 + 2 * z) * 512:(4 + 2 * z) * 512]
                    nm = 0
                    for fh in range(2):
                        for hc in range(HID0 // 128):
                            w0 = []
                            if fh == 0 and hc == 0:
                                w0 = [(SC, nh1_sc), (SC, h_zone_sc[z])]
                            nm = k.op("tensor", "matmul", zone[:, fh * 512:(fh + 1) * 512],
                                      h1b[:, hc * 512 + tt * 128:hc * 512 + tt * 128 + 128],
                                      w2b[:, hc * D + fh * 512:hc * D + (fh + 1) * 512],
                                      start=(hc == 0), stop=False, waits=w0, inc=MM)
                        nm = k.op("tensor", "matmul", zone[:, fh * 512:(fh + 1) * 512],
                                  fx["ones_row"][:], b2row[0:1, fh * 512:(fh + 1) * 512],
                                  start=False, stop=True, inc=MM)
                    h_h1_rd = nm
                    ti = t5 * 4 + tt
                    kwsc = {} if em < 0 else dict(scale=fx["W0"][:, ti * E0 + em:ti * E0 + em + 1])
                    nev = k.op("scalar", "activation", out=stg[z][:], in_=zone,
                               func=(AF.Identity if kwsc else AF.Copy),
                               waits=[(MM, nm), (GP, h_stg[z])], inc=SC, **kwsc)
                    h_zone_sc[z] = nev
                    kwa = {} if em < 0 else dict(accum_op=ALU.add)
                    h_stg[z] = k.op("gpsimd", "dma_start",
                                    out=dram["gacc"][ti * 128:(ti + 1) * 128, :],
                                    in_=stg[z][:],
                                    waits=[(SC, nev), (GP, h_gacc[ti])], inc=GP, **kwa)
                    h_gacc[ti] = h_stg[z]
            h_w_rd = MM.n
        k.barrier()
    mk.cur_ctx = None



def phase_C1(mk):
    k, nc, fx, dram = mk.k, mk.nc, mk.fixed, mk.dram
    LD, GP, MM, VE, SC = k.s("LD"), k.s("GP"), k.s("MM"), k.s("VE"), k.s("SC")
    PS = mk.PS
    NH = HID1 // 128
    from contextlib import ExitStack
    ctx = ExitStack()
    mk.cur_ctx = ctx
    with ctx:
        wgb = mk.sb("C1_wg", (128, ND * HID1), F32R)
        wub = mk.sb("C1_wu", (128, ND * HID1), F32R)
        wdb = mk.sb("C1_wd", (128, NH * D), F32R)
        gsb = mk.sb("C1_gs", (128, 512), F32)
        msb = mk.sb("C1_m", (128, NH * 512), F32R)
        xch = mk.sb("C1_xch", (128, ND * 512), F32R)
        stg = [mk.sb(f"C1_stg{j}", (128, D), F32) for j in (0, 1)]

        h_w_rd = 0
        h_x_rd = 0
        h_m_rd = 0
        h_gs_rd = 0
        h_stg = [0, 0]
        h_zone_sc = [0, 0]
        h_gacc = [0] * NT
        h_m_ve = 0
        first = True
        for g in (0, 1):
            W1 = fx[f"W1{g}"]
            for e in range(E1):
                nw = k.op("sync", "dma_start", out=wgb[:].rearrange("p (c f) -> p c f", c=ND),
                          in_=dram[f"l1_wg{g}"][e].rearrange("(c p) f -> p c f", p=128),
                          waits=[(MM, h_w_rd)], inc=LD)
                nw = k.op("sync", "dma_start", out=wub[:].rearrange("p (c f) -> p c f", c=ND),
                          in_=dram[f"l1_wu{g}"][e].rearrange("(c p) f -> p c f", p=128), inc=LD)
                nw = k.op("sync", "dma_start", out=wdb[:].rearrange("p (c f) -> p c f", c=NH),
                          in_=dram[f"l1_wd{g}"][e].rearrange("(c p) f -> p c f", p=128), inc=LD)
                for t5 in range(8):
                    tok0 = t5 * 512
                    nx = 0
                    for c in range(ND):
                        nx = k.op("sync", "dma_start", out=xch[:, c * 512:(c + 1) * 512],
                                  in_=dram["xnT"][c * 128:(c + 1) * 128, tok0:tok0 + 512],
                                  waits=[(MM, h_x_rd)] if c == 0 else (), inc=LD)
                    nm_ve = 0
                    for hp in range(NH):
                        nmg = 0
                        nmu_ = 0
                        for c in range(ND):
                            w0 = [(SC, SC.n), (VE, h_gs_rd), (LD, max(nw, nx))] if c == 0 else ()
                            nmg = k.op("tensor", "matmul", PS[:, 0:512],
                                       wgb[:, c * HID1 + hp * 128:c * HID1 + (hp + 1) * 128],
                                       xch[:, c * 512:(c + 1) * 512],
                                       start=(c == 0), stop=(c == ND - 1), waits=w0, inc=MM)
                        for c in range(ND):
                            w0 = [(VE, h_m_ve)] if c == 0 else ()
                            nmu_ = k.op("tensor", "matmul", PS[:, 512:1024],
                                        wub[:, c * HID1 + hp * 128:c * HID1 + (hp + 1) * 128],
                                        xch[:, c * 512:(c + 1) * 512],
                                        start=(c == 0), stop=(c == ND - 1), waits=w0, inc=MM)
                        nsi = k.op("scalar", "activation", out=gsb[:], in_=PS[:, 0:512],
                                   func=AF.Silu, waits=[(MM, nmg), (VE, h_gs_rd)], inc=SC)
                        h_m_ve = k.op("vector", "tensor_tensor",
                                      out=msb[:, hp * 512:(hp + 1) * 512],
                                      in0=PS[:, 512:1024], in1=gsb[:], op=ALU.mult,
                                      waits=[(SC, nsi), (MM, max(nmu_, h_m_rd))], lowp=True, inc=VE)
                        h_gs_rd = h_m_ve
                    h_x_rd = MM.n
                    for tt in range(4):
                        z = tt % 2
                        zone = PS[:, (2 + 2 * z) * 512:(4 + 2 * z) * 512]
                        nm = 0
                        for fh in range(2):
                            for hc in range(NH):
                                w0 = []
                                if fh == 0 and hc == 0:
                                    w0 = [(VE, h_m_ve), (SC, h_zone_sc[z])]
                                nm = k.op("tensor", "matmul", zone[:, fh * 512:(fh + 1) * 512],
                                          msb[:, hc * 512 + tt * 128:hc * 512 + tt * 128 + 128],
                                          wdb[:, hc * D + fh * 512:hc * D + (fh + 1) * 512],
                                          start=(hc == 0), stop=(hc == NH - 1), waits=w0, inc=MM)
                        h_m_rd = nm
                        ti = t5 * 4 + tt
                        nev = k.op("scalar", "activation", out=stg[z][:], in_=zone, func=AF.Identity,
                                   scale=W1[:, ti * E1 + e:ti * E1 + e + 1],
                                   waits=[(MM, nm), (GP, h_stg[z])], inc=SC)
                        h_zone_sc[z] = nev
                        kwa = {} if first else dict(accum_op=ALU.add)
                        h_stg[z] = k.op("gpsimd", "dma_start",
                                        out=dram["gacc"][ti * 128:(ti + 1) * 128, :],
                                        in_=stg[z][:],
                                        waits=[(SC, nev), (GP, h_gacc[ti])], inc=GP, **kwa)
                        h_gacc[ti] = h_stg[z]
                    if t5 == 7:
                        first = False
                h_w_rd = MM.n
            if g == 0:
                first = False
        k.barrier()
    mk.cur_ctx = None



def phase_D(mk, L):
    k, nc, fx, dram = mk.k, mk.nc, mk.fixed, mk.dram
    LD, GP, MM, VE, SC = k.s("LD"), k.s("GP"), k.s("MM"), k.s("VE"), k.s("SC")
    SQ = k.s("SQ")
    from contextlib import ExitStack
    ctx = ExitStack()
    mk.cur_ctx = ctx
    with ctx:
        xb = [mk.sb(f"D_x{j}_{L}", (128, D), F32) for j in (0, 1)]
        sb_ = [mk.sb(f"D_s{j}_{L}", (128, D), F32) for j in (0, 1)]
        gb = [mk.sb(f"D_g{j}_{L}", (128, D), F32) for j in (0, 1)]
        gg = [mk.sb(f"D_gg{j}_{L}", (128, D), F32) for j in (0, 1)]
        tb = [mk.sb(f"D_t{j}_{L}", (128, D), F32) for j in (0, 1)]
        ob = [mk.sb(f"D_o{j}_{L}", (128, D), F32) for j in (0, 1)]
        xsrc = dram["x0"] if L == 0 else dram["x1"]
        ssrc = dram["sfin"] if L == 0 else dram["sacc"]
        osink = dram["x1"] if L == 0 else dram["out"]
        h_t = [0, 0]
        h_no = [0, 0]
        h_odma = [0, 0]
        h_g = [0, 0]
        for i in range(NT):
            p = i % 2
            r0 = i * 128
            w0 = [(VE, h_no[p])]
            if i < 2:
                w0.append((GP, k.s("GP").n))
                w0.append((SQ, k.s("SQ").n))
            n1 = k.op("sync", "dma_start", out=xb[p][:], in_=xsrc[r0:r0 + 128, :],
                      waits=w0, inc=LD)
            n2 = k.op("sync", "dma_start", out=sb_[p][:], in_=ssrc[r0:r0 + 128, :], inc=LD)
            n3 = k.op("sync", "dma_start", out=gb[p][:], in_=dram["gacc"][r0:r0 + 128, :],
                      waits=[(SC, h_g[p])], inc=LD)
            nsi = k.op("scalar", "activation", out=gg[p][:], in_=gb[p][:], func=AF.Silu,
                       waits=[(LD, n3), (VE, h_t[p])], inc=SC)
            h_g[p] = nsi
            nt = k.op("vector", "tensor_tensor", out=tb[p][:], in0=sb_[p][:], in1=gg[p][:],
                      op=ALU.mult, waits=[(SC, nsi), (LD, n2)], inc=VE)
            h_t[p] = nt
            k.wait("vector", VE, nt)
            no = k.op("vector", "tensor_tensor", out=ob[p][:], in0=xb[p][:], in1=tb[p][:],
                      op=ALU.add, waits=[(LD, h_odma[p])], inc=VE)
            h_no[p] = no
            h_odma[p] = k.op("sync", "dma_start", out=osink[r0:r0 + 128, :], in_=ob[p][:],
                             waits=[(VE, no)], inc=LD)
        k.barrier()
    mk.cur_ctx = None



def kernel(**inputs):
    from concourse.bass_utils import run_bass_kernel_spmd
    shared = host_prepare(inputs)
    x = np.ascontiguousarray(np.asarray(inputs["L0_attn_x"], dtype=np.float32))
    B = x.shape[0]
    nc = build(layers=(0, 1), dbg=False)
    in_maps = []
    for b in range(B):
        im = dict(shared)
        im["x0"] = np.ascontiguousarray(x[b])
        in_maps.append(im)
    res = run_bass_kernel_spmd(nc, in_maps, core_ids=list(range(B)))
    out = np.stack([np.asarray(res.results[b]["out"], dtype=np.float32) for b in range(B)], axis=0)
    return out


# version 02
# speedup vs baseline: 1.0292x; 1.0292x over previous
import numpy as np
import concourse.bass as bass
import concourse.mybir as mybir

F32 = mybir.dt.float32
F32R = mybir.dt.float32r
BF16 = mybir.dt.bfloat16
U32 = mybir.dt.uint32
AF = mybir.ActivationFunctionType
ALU = mybir.AluOpType
AXX = mybir.AxisListType.X

T, D, H, HD = 4096, 1024, 16, 64
NT = T // 128
ND = D // 128
NP = H // 2
CAP = 1024
E1, HID1 = 32, 512
E0, HID0 = 8, 2048
EPS = 1e-5
SCALE = HD ** -0.5

ENGINES = ("tensor", "vector", "scalar", "gpsimd", "sync")


class Sem:
    def __init__(self, handle, name):
        self.h = handle
        self.name = name
        self.n = 0


class K:
    def __init__(self, nc, ctx):
        self.nc = nc
        self.ctx = ctx
        self.q = {e: [] for e in ENGINES}
        self.sems = {}
        for name in ("LD", "GP", "SQ", "MM", "VE", "SC", "GC", "SY"):
            self.sems[name] = Sem(ctx.enter_context(nc.semaphore(name)), name)

    def s(self, name):
        return self.sems[name]

    def op(self, eng, fname, *a, waits=(), inc=None, inc_n=None, lowp=False, **kw):
        if inc is not None:
            if inc_n is None:
                inc_n = 16 if "dma" in fname else 1
            inc.n += inc_n
            post = inc.n
        else:
            post = None
        waits = tuple((s, v) for (s, v) in waits if v > 0)
        nc = self.nc

        import traceback
        site = "".join(traceback.format_stack(limit=3)[:-1])

        def fn(E, a=a, kw=kw, waits=waits, inc=inc, inc_n=inc_n, fname=fname, lowp=lowp, site=site):
            for s, v in waits:
                E.wait_ge(s.h, v)
            try:
                if lowp:
                    with nc.allow_low_precision(reason="intended f32r/bf16 rounding"):
                        ins = getattr(E, fname)(*a, **kw)
                else:
                    ins = getattr(E, fname)(*a, **kw)
            except Exception:
                info = []
                for kk, vv in kw.items():
                    try:
                        info.append(f"{kk}={getattr(vv, 'name', None) or ''}{getattr(vv, 'shape', '')} off={getattr(vv, 'offset', '')}")
                    except Exception:
                        info.append(f"{kk}=?")
                print("FAILED op", fname, "args:", "; ".join(str(x) for x in info), "emitted at:\n", site)
                raise
            if inc is not None:
                if ins is None:
                    ins = E.nop()
                ins.then_inc(inc.h, inc_n)
        self.q[eng].append(fn)
        return post

    def drain_pad(self, eng, scratch_ap):
        if eng == "vector":
            self.op(eng, "memset", scratch_ap, 0.0, inc=self.s("VE"))
        else:
            self.op(eng, "activation", out=scratch_ap, in_=scratch_ap,
                    func=AF.Copy, inc=self.s("SC"))

    def settle(self, eng, sem, n):
        self.wait(eng, sem, n)
        return self.op(eng, "nop", inc=sem)

    def wait(self, eng, sem, val):
        if val <= 0:
            return

        def fn(E, sem=sem, val=val):
            E.wait_ge(sem.h, val)
        self.q[eng].append(fn)

    def barrier(self):
        prog = {"tensor": "MM", "vector": "VE", "scalar": "SC",
                "gpsimd": "GC", "sync": "SY"}
        marks = {}
        for e in ENGINES:
            s = self.s(prog[e])
            marks[e] = self.op(e, "nop", inc=s)
        for e in ENGINES:
            for e2 in ENGINES:
                if e2 != e:
                    self.wait(e, self.s(prog[e2]), marks[e2])

    def finalize(self, block):
        for e in ENGINES:
            ops = self.q[e]
            getattr(block, e)(lambda E, ops=ops: [fn(E) for fn in ops])



def host_prepare(inp):
    f = np.ascontiguousarray
    d = {}
    d["l0_wqT"] = f(inp["L0_attn_Wq"].T)
    d["l0_wkT"] = f(inp["L0_attn_Wk"].T)
    d["l0_wvT"] = f(inp["L0_attn_Wv"].T)
    d["l0_woT"] = f(inp["L0_attn_Wo"].T)
    d["l0_bq"] = f(inp["L0_attn_bq"].reshape(ND, 128).T)
    d["l0_bk"] = f(inp["L0_attn_bk"].reshape(ND, 128).T)
    d["l0_bv_row"] = f(inp["L0_attn_bv"].reshape(1, D))
    d["l0_bo_row"] = f(inp["L0_attn_bo"].reshape(1, D))
    d["l0_arep"] = f(np.repeat(np.asarray(inp["L0_attn_adaptive_scale"]), HD).reshape(ND, 128).T)
    d["l0_gamma_row"] = f(inp["L0_attn_gamma"].reshape(1, D))
    d["l0_beta_row"] = f(inp["L0_attn_beta"].reshape(1, D))
    d["l0_rw"] = f(inp["L0_ffn_router_w"])
    d["l0_rb_row"] = f(inp["L0_ffn_router_b"].reshape(1, E0))
    d["l0_sw1"] = f(inp["L0_ffn_sw1"])
    d["l0_sb1"] = f(inp["L0_ffn_sb1"].reshape(HID0 // 128, 128).T)
    d["l0_sw2"] = f(inp["L0_ffn_sw2"])
    d["l0_sb2_row"] = f(inp["L0_ffn_sb2"].reshape(1, D))
    d["l0_ew1"] = f(inp["L0_ffn_rw1"])
    d["l0_eb1"] = f(np.transpose(inp["L0_ffn_rb1"].reshape(E0, HID0 // 128, 128), (0, 2, 1)))
    d["l0_ew2"] = f(inp["L0_ffn_rw2"])
    d["l0_eb2"] = f(inp["L0_ffn_rb2"].reshape(E0, 1, D))
    d["l1_wq"] = f(inp["L1_attn_Wq"])
    d["l1_bq"] = f(inp["L1_attn_bq"].reshape(ND, 128).T)
    d["l1_wl"] = f(inp["L1_attn_Wl"])
    d["l1_bl"] = f(inp["L1_attn_bl"].reshape(2, 128).T)
    d["l1_wk"] = f(inp["L1_attn_Wk"])
    d["l1_bk"] = f(inp["L1_attn_bk"].reshape(ND, 128).T)
    d["l1_wv"] = f(inp["L1_attn_Wv"])
    d["l1_bv_row"] = f(inp["L1_attn_bv"].reshape(1, D))
    d["l1_wo"] = f(inp["L1_attn_Wo"])
    d["l1_bo_row"] = f(inp["L1_attn_bo"].reshape(1, D))
    d["l1_rw0"] = f(inp["L1_ffn_router_w0"])
    d["l1_rw1"] = f(inp["L1_ffn_router_w1"])
    for g, pre in ((0, "0"), (1, "1")):
        d[f"l1_wg{pre}"] = f(inp[f"L1_ffn_wg{pre}"])
        d[f"l1_wu{pre}"] = f(inp[f"L1_ffn_wu{pre}"])
        d[f"l1_wd{pre}"] = f(inp[f"L1_ffn_wd{pre}"])
    d["ident"] = np.eye(128, dtype=np.float32)
    d["triu"] = np.triu(np.ones((128, 128), np.float32), 1)
    d["ones_col"] = np.ones((128, 1), np.float32)
    d["ones_row"] = np.ones((1, 128), np.float32)
    d["iota_row"] = np.arange(E1, dtype=np.float32).reshape(1, E1)
    d["ecap_row"] = (np.arange(E1, dtype=np.float32) * CAP).reshape(1, E1)
    cs = np.zeros((128, 8), np.float32)
    cs[:, 0] = EPS
    cs[:, 1] = 1e-30
    cs[:, 2] = SCALE
    cs[:, 3] = 1.0
    d["consts"] = cs
    return d



class MK:
    def __init__(self, nc, ctx, k, dbg=False):
        self.nc = nc
        self.ctx = ctx
        self.k = k
        self.dbg = dbg
        self.dram = {}
        self.fixed = {}

    def din(self, name, shape, dtype):
        self.dram[name] = self.nc.dram_tensor(name, list(shape), dtype, kind="ExternalInput").ap()
        return self.dram[name]

    def dscratch(self, name, shape, dtype):
        if name in getattr(self, "dbg_in", ()):
            kind = "ExternalInput"
        elif self.dbg:
            kind = "ExternalOutput"
        else:
            kind = "Internal"
        self.dram[name] = self.nc.dram_tensor(name, list(shape), dtype, kind=kind).ap()
        return self.dram[name]

    def dout(self, name, shape, dtype):
        self.dram[name] = self.nc.dram_tensor(name, list(shape), dtype, kind="ExternalOutput").ap()
        return self.dram[name]

    def sb(self, name, shape, dtype):
        ctx = getattr(self, "cur_ctx", None) or self.ctx
        t = ctx.enter_context(self.nc.sbuf_tensor(name, list(shape), dtype))
        return t

    def declare_dram(self):
        din = self.din
        din("x0", (T, D), F32)
        for n, sh in (("l0_wqT", (D, D)), ("l0_wkT", (D, D)), ("l0_wvT", (D, D)),
                      ("l0_woT", (D, D)), ("l0_rw", (D, E0)), ("l0_sw1", (D, HID0)),
                      ("l0_sw2", (HID0, D)), ("l0_ew1", (E0, D, HID0)),
                      ("l0_ew2", (E0, HID0, D)),
                      ("l1_wq", (D, D)), ("l1_wl", (D, 256)), ("l1_wk", (256, D)),
                      ("l1_wv", (256, D)), ("l1_wo", (D, D)),
                      ("l1_rw0", (D, E1)), ("l1_rw1", (D, E1))):
            din(n, sh, F32R)
        for n in ("l0_bq", "l0_bk", "l0_arep", "l0_sb1", "l1_bq", "l1_bl", "l1_bk", "consts"):
            din(n, (128, {"l0_sb1": 16, "l1_bl": 2, "consts": 8}.get(n, ND)), F32)
        for n in ("l0_bv_row", "l0_bo_row", "l0_gamma_row", "l0_beta_row", "l0_sb2_row",
                  "l1_bv_row", "l1_bo_row"):
            din(n, (1, D), F32R)
        din("l0_rb_row", (1, E0), F32R)
        din("l0_eb1", (E0, 128, 16), F32)
        din("l0_eb2", (E0, 1, D), F32R)
        for g in (0, 1):
            din(f"l1_wg{g}", (E1, D, HID1), F32R)
            din(f"l1_wu{g}", (E1, D, HID1), F32R)
            din(f"l1_wd{g}", (E1, HID1, D), F32R)
        for n in ("ident", "triu"):
            din(n, (128, 128), F32R)
        din("ones_col", (128, 1), F32R)
        din("ones_row", (1, 128), F32R)
        din("iota_row", (1, E1), F32R)
        din("ecap_row", (1, E1), F32R)
        ds = self.dscratch
        ds("xnT", (D, T), F32R)
        ds("xn_tok", (T, D), F32)
        ds("qT", (D, T), F32R)
        ds("kT", (D, T), F32R)
        ds("vtok", (T, 1040), F32R)
        ds("sacc", (T, D), F32)
        ds("sfin", (T, D), F32)
        ds("gacc", (T, D), F32)
        ds("x1", (T, D), F32)
        for g in (0, 1):
            ds(f"cap{g}", (E1 * CAP, D), BF16)
            ds(f"res{g}", (E1 * CAP, D), F32)
        self.dout("out", (T, D), F32)

    def load_fixed(self):
        k, sb = self.k, self.sb
        LD = k.s("LD")
        fx = self.fixed
        fx["ident"] = sb("fx_ident", (128, 128), F32R)
        fx["triu"] = sb("fx_triu", (128, 128), F32R)
        fx["ones_col"] = sb("fx_ones_col", (128, 1), F32R)
        fx["ones_row"] = sb("fx_ones_row", (1, 128), F32R)
        fx["iota_row"] = sb("fx_iota_row", (1, E1), F32R)
        fx["ecap_row"] = sb("fx_ecap_row", (1, E1), F32R)
        fx["consts"] = sb("fx_consts", (128, 8), F32)
        fx["W0"] = sb("fx_W0", (128, NT * E0), F32)
        fx["RBC"] = sb("fx_RBC", (128, E0), F32)
        fx["W10"] = sb("fx_W10", (128, NT * E1), F32)
        fx["W11"] = sb("fx_W11", (128, NT * E1), F32)
        fx["RT"] = sb("fx_RT", (128, 2 * NT * 4), F32)
        fx["SLOT"] = sb("fx_SLOT", (128, 2 * NT * 2), U32)
        fx["bias"] = sb("fx_bias", (128, 64), F32)
        fx["rows"] = sb("fx_rows", (1, 64), F32)
        for name in ("W0", "RT", "W10", "W11", "SLOT", "bias", "rows", "RBC"):
            k.op("vector", "memset", fx[name][:], 0, inc=k.s("VE"))
        n = 0
        for name in ("ident", "triu", "ones_col", "ones_row", "iota_row", "ecap_row", "consts"):
            n = k.op("sync", "dma_start", out=fx[name][:], in_=self.dram[name], inc=LD)
        self.fixed_loaded = n


def relerr(a, b):
    a = np.asarray(a, np.float64)
    b = np.asarray(b, np.float64)
    return np.linalg.norm(a - b) / (np.linalg.norm(b) + 1e-12)



def layer_prep(mk, L):
    k, nc, fx, dram = mk.k, mk.nc, mk.fixed, mk.dram
    LD, GP, MM, VE, SC = (k.s(x) for x in ("LD", "GP", "MM", "VE", "SC"))
    if "l0rw" not in fx:
        fx["l0rw"] = mk.sb("fx_l0rw", (128, E0 * ND), F32R)
        fx["l1rw"] = mk.sb("fx_l1rw", (128, 2 * E1 * ND), F32R)
        fx["IOT"] = mk.sb("fx_IOT", (128, E1), F32)
        fx["base"] = mk.sb("fx_base", (1, 64), F32R)
        fx["bcol"] = mk.sb("fx_bcol", (128, 48), F32)
        fx["rbrow"] = mk.sb("fx_rbrow", (1, E0), F32R)
    w = {}
    if L == 0:
        w["rw"] = k.op("sync", "dma_start",
                       out=fx["l0rw"][:].rearrange("p (c e) -> p c e", c=ND),
                       in_=dram["l0_rw"].rearrange("(c p) e -> p c e", p=128), inc=LD)
        w["rb"] = k.op("sync", "dma_start", out=fx["rbrow"][:], in_=dram["l0_rb_row"], inc=LD)
        nrb = k.op("tensor", "matmul", mk.PS[:, 8:16], fx["ones_row"][:], fx["rbrow"][:],
                   start=True, stop=True, waits=[(LD, w["rb"]), (LD, mk.fixed_loaded)], inc=MM)
        w["rbc"] = k.op("scalar", "activation", out=fx["RBC"][:], in_=mk.PS[:, 8:16],
                        func=AF.Copy, waits=[(MM, nrb)], inc=SC)
        k.op("sync", "dma_start", out=fx["bcol"][:, 0:8], in_=dram["l0_bq"], inc=LD)
        k.op("sync", "dma_start", out=fx["bcol"][:, 8:16], in_=dram["l0_bk"], inc=LD)
        n = k.op("sync", "dma_start", out=fx["bcol"][:, 16:24], in_=dram["l0_arep"], inc=LD)
        n1 = k.op("vector", "tensor_scalar", out=fx["bcol"][:, 24:32], in0=fx["bcol"][:, 16:24],
                  scalar1=SCALE, scalar2=None, op0=ALU.mult, waits=[(LD, n)], inc=VE)
        k.wait("vector", VE, n1)
        w["qs"] = k.op("vector", "tensor_tensor", out=fx["bcol"][:, 32:40], in0=fx["bcol"][:, 0:8],
                       in1=fx["bcol"][:, 24:32], op=ALU.mult, inc=VE)
    else:
        for g in (0, 1):
            w[f"rw{g}"] = k.op("sync", "dma_start",
                               out=fx["l1rw"][:, g * 256:(g + 1) * 256].rearrange("p (c e) -> p c e", c=ND),
                               in_=dram[f"l1_rw{g}"].rearrange("(c p) e -> p c e", p=128), inc=LD)
        k.op("sync", "dma_start", out=fx["bcol"][:, 0:8], in_=dram["l1_bq"], inc=LD)
        k.op("sync", "dma_start", out=fx["bcol"][:, 16:18], in_=dram["l1_bl"], inc=LD)
        n = k.op("sync", "dma_start", out=fx["bcol"][:, 8:16], in_=dram["l1_bk"], inc=LD)
        w["qs"] = k.op("vector", "tensor_scalar", out=fx["bcol"][:, 32:40], in0=fx["bcol"][:, 0:8],
                       scalar1=SCALE, scalar2=None, op0=ALU.mult, waits=[(LD, n)], inc=VE)
        if True:
            nm = k.op("tensor", "matmul", mk.PS[:, 0:E1], fx["ones_row"][:], fx["iota_row"][:],
                      start=True, stop=True, waits=[(LD, mk.fixed_loaded)], inc=MM)
            w["iot"] = k.op("scalar", "activation", out=fx["IOT"][:], in_=mk.PS[:, 0:E1],
                            func=AF.Copy, waits=[(MM, nm)], inc=SC)
            nz = k.op("vector", "memset", fx["rows"][0:1, 0:64], 0.0, inc=VE)
            w["base"] = k.op("scalar", "activation", out=fx["base"][:], in_=fx["rows"][0:1, 0:64],
                             func=AF.Copy, waits=[(VE, nz)], inc=SC)
    return w


def phase_A(mk, L, prep):
    k, nc, fx, dram = mk.k, mk.nc, mk.fixed, mk.dram
    LD, GP, MM, VE, SC = (k.s(x) for x in ("LD", "GP", "MM", "VE", "SC"))
    PS = mk.PS
    xsrc = dram["x0"] if L == 0 else dram["x1"]
    from contextlib import ExitStack
    ctx = ExitStack()
    mk.cur_ctx = ctx
    with ctx:
        xt = [mk.sb(f"A_xt{j}_{L}", (128, D), F32) for j in (0, 1)]
        xnt = [mk.sb(f"A_xnt{j}_{L}", (128, D), F32) for j in (0, 1)]

        stage = [mk.sb(f"A_stg{j}_{L}", (128, D), F32R) for j in (0, 1)]
        stats = mk.sb(f"A_stats_{L}", (128, 48), F32)
        k.op("vector", "memset", stats[:], 0.0, inc=k.s("VE"))
        scrV = mk.sb(f"A_scrV_{L}", (128, D), F32)
        scrS = mk.sb(f"A_scrS_{L}", (128, D), F32)
        rsc = mk.sb(f"A_rsc_{L}", (128, 112), F32)
        k.op("vector", "memset", rsc[:], 0.0, inc=k.s("VE"))
        mk._rsc_tmp = rsc
        rscR = mk.sb(f"A_rscR_{L}", (128, 96), F32R)
        h_scat = [0, 0]
        iotn = max(prep.get("iot", 0), prep.get("rbc", 0))
        h_evac = [iotn, iotn]
        h_route = [0, 0]
        h_store = [0, 0]
        h_xn = [0, 0]
        h_sqrd = [0]
        h_tp = [0, 0]
        h_prb = 0
        h_rankmm = 0
        h_sq = [0, 0]
        h_rpsum = [0, 0]
        h_rpsum_sc = [0, 0]
        h_rk = [0, 0]
        n_xnt_dma = [0, 0]
        for i in range(NT):
            p = i % 2
            so = (i % 2) * 24
            r0 = i * 128
            nx = k.op("sync", "dma_start", out=xt[p][:], in_=xsrc[r0:r0 + 128, :],
                      waits=[(VE, h_xn[p]), (SC, h_sq[p])], inc=LD)
            nsq0 = k.op("scalar", "activation", out=scrS[:], in_=xt[p][:],
                        func=AF.Square, waits=[(LD, nx), (VE, h_sqrd[0])], inc=SC)
            for q_ in range(4):
                sl = slice(q_ * 256, (q_ + 1) * 256)
                nsum = k.op("vector", "tensor_reduce", out=stats[:, so + 8 + q_:so + 9 + q_],
                            in_=xt[p][:, sl], op=ALU.add, axis=AXX,
                            waits=[(LD, nx)] if q_ == 0 else (), inc=VE)
                nsq = k.op("vector", "tensor_reduce", out=stats[:, so + 12 + q_:so + 13 + q_],
                            in_=scrS[:, sl], op=ALU.add, axis=AXX,
                            waits=[(SC, nsq0)] if q_ == 0 else (), inc=VE)
            h_sqrd[0] = nsq
            k.wait("vector", VE, VE.n)
            k.op("vector", "tensor_tensor", out=stats[:, so + 16:so + 18],
                 in0=stats[:, so + 8:so + 10], in1=stats[:, so + 10:so + 12], op=ALU.add, inc=VE)
            nlv = k.op("vector", "tensor_tensor", out=stats[:, so + 18:so + 20],
                 in0=stats[:, so + 12:so + 14], in1=stats[:, so + 14:so + 16], op=ALU.add,
                 inc=VE)
            k.wait("vector", VE, nlv)
            nsum = k.op("vector", "tensor_tensor", out=stats[:, so:so + 1],
                        in0=stats[:, so + 16:so + 17], in1=stats[:, so + 17:so + 18], op=ALU.add, inc=VE)
            nsq = k.op("vector", "tensor_tensor", out=stats[:, so + 1:so + 2],
                       in0=stats[:, so + 18:so + 19], in1=stats[:, so + 19:so + 20], op=ALU.add, inc=VE)
            k.wait("vector", VE, nsq)
            nmu = k.op("vector", "tensor_scalar", out=stats[:, so + 2:so + 3],
                       in0=stats[:, so:so + 1], scalar1=1.0 / D, scalar2=None, op0=ALU.mult,
                       inc=VE)
            k.wait("vector", VE, nmu)
            nmu2 = k.op("vector", "tensor_tensor", out=stats[:, so + 3:so + 4],
                        in0=stats[:, so + 2:so + 3], in1=stats[:, so + 2:so + 3], op=ALU.mult,
                        inc=VE)
            k.wait("vector", VE, nmu2)
            nvar = k.op("vector", "scalar_tensor_tensor", out=stats[:, so + 4:so + 5],
                        in0=stats[:, so + 1:so + 2], scalar=1.0 / D, in1=stats[:, so + 3:so + 4],
                        op0=ALU.mult, op1=ALU.subtract, inc=VE)
            nln = k.op("scalar", "activation", out=stats[:, so + 5:so + 6], in_=stats[:, so + 4:so + 5],
                       func=AF.Ln, bias=fx["consts"][:, 0:1], waits=[(VE, nvar), (LD, mk.fixed_loaded)],
                       inc=SC)
            k.wait("scalar", SC, nln)
            nrstd = k.op("scalar", "activation", out=stats[:, so + 6:so + 7], in_=stats[:, so + 5:so + 6],
                         func=AF.Exp, scale=-0.5, inc=SC)
            xnw = [(SC, nrstd), (MM, h_tp[p])]
            if L == 0:
                xnw.append((LD, n_xnt_dma[p]))
            nxn = k.op("vector", "tensor_scalar", out=xnt[p][:], in0=xt[p][:],
                       scalar1=stats[:, so + 2:so + 3], scalar2=stats[:, so + 6:so + 7],
                       op0=ALU.subtract, op1=ALU.mult, waits=xnw, lowp=True, inc=VE)
            h_xn[p] = nxn
            h_sq[p] = nsq0

            if L == 0:
                n_xnt_dma[p] = k.op("sync", "dma_start", out=dram["xn_tok"][r0:r0 + 128, :],
                                    in_=xnt[p][:], waits=[(VE, nxn)], inc=LD)
            ntp = 0
            for c in range(ND):
                bank = 2 * p + c // 4
                off = bank * 512 + (c % 4) * 128
                ntp = k.op("tensor", "transpose", PS[:, off:off + 128],
                           xnt[p][:, c * 128:(c + 1) * 128], fx["ident"][:].bitcast(F32),
                           waits=[(VE, nxn), (SC, h_evac[p]), (LD, mk.fixed_loaded)], inc=MM)
            h_tp[p] = ntp
            nev = 0
            for c in range(ND):
                bank = 2 * p + c // 4
                off = bank * 512 + (c % 4) * 128
                wmm = ntp if c >= 4 else ntp - 4
                nev = k.op("scalar", "activation", out=stage[p][:, c * 128:(c + 1) * 128],
                           in_=PS[:, off:off + 128], func=AF.Copy,
                           waits=[(MM, max(wmm, h_route[p])), (LD, h_store[p])], inc=SC)
            h_evac[p] = nev
            nst = 0
            for c in range(ND):
                nst = k.op("sync", "dma_start",
                           out=dram["xnT"][c * 128:(c + 1) * 128, r0:r0 + 128],
                           in_=stage[p][:, c * 128:(c + 1) * 128],
                           waits=[(SC, nev)] if c == 0 else (), inc=LD)
            h_store[p] = nst
            if L == 0:
                rb = 4 + p
                rz = rb * 512
                nmmr = 0
                for c in range(ND):
                    nmmr = k.op("tensor", "matmul", PS[:, rz:rz + E0],
                                stage[p][:, c * 128:(c + 1) * 128],
                                fx["l0rw"][:, c * E0:(c + 1) * E0],
                                start=(c == 0), stop=(c == ND - 1),
                                waits=[(SC, nev), (VE, h_rpsum[p]), (SC, h_rpsum_sc[p])] if c == 0 else [(SC, nev)],
                                inc=MM)
                h_route[p] = nmmr
                if i >= NT - 2:
                    k.wait("vector", VE, VE.n)
                    k.drain_pad("vector", scrV[:, 0:512])
                nlg = k.op("vector", "tensor_tensor", out=rsc[:, 56:64], in0=PS[:, rz:rz + E0],
                           in1=fx["RBC"][:], op=ALU.add,
                           waits=[(MM, nmmr), (SC, prep["rbc"]), (SC, h_prb)], inc=VE)
                h_rpsum[p] = nlg
                h_rpsum_sc[p] = 0
                k.wait("vector", VE, nlg)
                nm8 = k.op("vector", "max", out=rsc[:, 0:8], in_=rsc[:, 56:64], inc=VE)
                k.wait("vector", VE, nm8)
                nneg = k.op("vector", "tensor_scalar", out=rsc[:, 8:9], in0=rsc[:, 0:1],
                            scalar1=-1.0, scalar2=None, op0=ALU.mult, inc=VE)
                nprb = k.op("scalar", "activation", out=rsc[:, 16:24], in_=rsc[:, 56:64],
                            func=AF.Exp, bias=rsc[:, 8:9], waits=[(VE, nneg)], inc=SC)
                h_prb = nprb
                nmsk = k.op("vector", "tensor_scalar", out=rsc[:, 24:32], in0=rsc[:, 56:64],
                            scalar1=rsc[:, 1:2], scalar2=None, op0=ALU.is_ge,
                            waits=[(SC, nprb)], inc=VE)
                k.wait("vector", VE, nmsk)
                nwu = k.op("vector", "tensor_tensor", out=rsc[:, 32:40], in0=rsc[:, 16:24],
                           in1=rsc[:, 24:32], op=ALU.mult, inc=VE)
                nls = k.op("vector", "tensor_reduce", out=rsc[:, 9:10], in_=rsc[:, 16:24],
                           op=ALU.add, axis=AXX, waits=[(SC, nprb)], inc=VE)
                k.wait("vector", VE, nls)
                nlr = k.op("vector", "reciprocal", rsc[:, 10:11], rsc[:, 9:10], inc=VE)
                k.wait("vector", VE, nlr)
                k.op("vector", "tensor_scalar", out=fx["W0"][:, i * E0:(i + 1) * E0],
                     in0=rsc[:, 32:40], scalar1=rsc[:, 10:11], scalar2=None,
                     op0=ALU.mult, inc=VE)
            else:
                for g in (0, 1):
                    rb = 4 + p
                    rz = rb * 512 + g * 64
                    nmmr = 0
                    for c in range(ND):
                        nmmr = k.op("tensor", "matmul", PS[:, rz:rz + E1],
                                    stage[p][:, c * 128:(c + 1) * 128],
                                    fx["l1rw"][:, g * 256 + c * E1:g * 256 + (c + 1) * E1],
                                    start=(c == 0), stop=(c == ND - 1),
                                    waits=[(SC, nev), (VE, h_rpsum[p]), (SC, h_rpsum_sc[p]),
                                           (LD, prep[f"rw{g}"])] if c == 0 else [(SC, nev)],
                                    inc=MM)
                    h_route[p] = nmmr
                    if i >= NT - 2:
                        k.wait("vector", VE, VE.n)
                        k.drain_pad("vector", scrV[:, 0:512])
                    nm8 = k.op("vector", "max", out=rsc[:, 0:8], in_=PS[:, rz:rz + E1],
                               waits=[(MM, nmmr)], inc=VE)
                    k.wait("vector", VE, nm8)
                    nneg = k.op("vector", "tensor_scalar", out=rsc[:, 8:9], in0=rsc[:, 0:1],
                                scalar1=-1.0, scalar2=None, op0=ALU.mult,
                                waits=[(SC, h_prb)], inc=VE)
                    nprb = k.op("scalar", "activation", out=rsc[:, 16:48], in_=PS[:, rz:rz + E1],
                                func=AF.Exp, bias=rsc[:, 8:9], waits=[(VE, nneg)], inc=SC)
                    h_prb = nprb
                    nmsk = k.op("vector", "tensor_scalar", out=rsc[:, 16 + 32:16 + 64], in0=PS[:, rz:rz + E1],
                                scalar1=rsc[:, 1:2], scalar2=None, op0=ALU.is_ge,
                                waits=[(SC, nprb)], inc=VE)
                    h_rpsum[p] = nmsk
                    h_rpsum_sc[p] = nprb
                    nls = k.op("vector", "tensor_reduce", out=rsc[:, 9:10], in_=rsc[:, 16:48],
                               op=ALU.add, axis=AXX, inc=VE)
                    k.wait("vector", VE, nls)
                    nwu = k.op("vector", "tensor_tensor", out=rsc[:, 80:112], in0=rsc[:, 16:48],
                               in1=rsc[:, 48:80], op=ALU.mult, inc=VE)
                    nlr = k.op("vector", "reciprocal", rsc[:, 10:11], rsc[:, 9:10], inc=VE)
                    k.wait("vector", VE, nlr)
                    k.op("vector", "tensor_scalar",
                         out=fx[f"W1{g}"][:, i * E1:(i + 1) * E1], in0=rsc[:, 80:112],
                         scalar1=rsc[:, 10:11], scalar2=0.5, op0=ALU.mult, op1=ALU.mult,
                         inc=VE)
        k.op("tensor", "matmul", PS[0:1, 0:2], fx["ones_row"][0:1, 0:1],
             fx["ones_row"][0:1, 0:2], start=True, stop=True,
             waits=[(SC, h_evac[0]), (SC, h_evac[1]), (VE, h_rpsum[0]),
                    (VE, h_rpsum[1])], inc=MM)
        if mk.dbg:
            ddr = mk.dout(f"dbgRSC{L}", (128, 112), F32)
            k.op("sync", "dma_start", out=ddr, in_=mk._rsc_tmp[:],
                 waits=[(k.s("VE"), k.s("VE").n), (k.s("SC"), k.s("SC").n)], inc=k.s("LD"))
            dst = mk.dout(f"dbgSTATS{L}", (128, 48), F32)
            k.op("sync", "dma_start", out=dst, in_=stats[:], inc=k.s("LD"))
        k.barrier()


def build(layers=(0, 1), phases=("A", "B0", "B", "B2", "C", "C1", "D"),
          dbg=False, dbg_in=(), nodetect=True):
    from contextlib import ExitStack
    nc = bass.Bass("TRN2", target_bir_lowering=False, detect_race_conditions=not nodetect)
    with ExitStack() as ctx:
        k = K(nc, ctx)
        mk = MK(nc, ctx, k, dbg=dbg)
        mk.dbg_in = set(dbg_in)
        mk.declare_dram()
        mk.PS = ctx.enter_context(nc.psum_tensor("PS", [128, 4096], F32))
        mk.cur_ctx = None
        mk.load_fixed()
        for L in layers:
            prep = layer_prep(mk, L)
            if "A" in phases:
                phase_A(mk, L, prep)
                mk.cur_ctx = None
            if "B0" in phases:
                phase_B0(mk, L, prep)
                mk.cur_ctx = None
            if "B" in phases:
                phase_B(mk, L)
                mk.cur_ctx = None
            if "B2" in phases and L == 0:
                phase_B2(mk)
                mk.cur_ctx = None
            if "C" in phases and L == 0:
                phase_C(mk)
                mk.cur_ctx = None
            if "C1" in phases and L == 1:
                phase_C1(mk)
                mk.cur_ctx = None
            if "D" in phases:
                phase_D(mk, L)
                mk.cur_ctx = None
        if dbg:
            fx = mk.fixed
            dd = mk.dout("dbgW0", (128, NT * E0), F32)

            dr = mk.dout("dbgRT", (128, 2 * NT * 4), F32)
            dsl = mk.dout("dbgSLOT", (128, 2 * NT * 2), U32)
            k.op("sync", "dma_start", out=dd, in_=fx["W0"][:], inc=k.s("LD"))
            k.op("sync", "dma_start", out=dr, in_=fx["RT"][:], inc=k.s("LD"))
            k.op("sync", "dma_start", out=dsl, in_=fx["SLOT"][:], inc=k.s("LD"))
        k.barrier()
        with nc.sbuf_tensor("final_flush", [128, D], F32) as ffl:
            k.op("gpsimd", "dma_start", out=ffl[:], in_=mk.dram["gacc"][T - 128:T, :],
                 waits=[(k.s("GP"), k.s("GP").n)], inc=k.s("GP"))
        k.wait("sync", k.s("LD"), k.s("LD").n)
        k.wait("sync", k.s("GP"), k.s("GP").n)
        k.wait("sync", k.s("SQ"), k.s("SQ").n)
        block = ctx.enter_context(nc.Block())
        k.finalize(block)
    return nc





def phase_B0(mk, L, prep=None):
    k, nc, fx, dram = mk.k, mk.nc, mk.fixed, mk.dram
    LD, GP, MM, VE, SC = (k.s(x) for x in ("LD", "GP", "MM", "VE", "SC"))
    PS = mk.PS
    from contextlib import ExitStack
    ctx = ExitStack()
    mk.cur_ctx = ctx
    with ctx:
        xnTh = mk.sb(f"B0_xnTh_{L}", (128, ND * 2048), F32R)
        wbuf = mk.sb(f"B0_wbuf_{L}", (128, 8192), F32R)
        latsb = mk.sb(f"B0_lat_{L}", (128, 2 * 2048), F32R) if L == 1 else None
        stg = [mk.sb(f"B0_stg{j}_{L}", (128, 1024), F32R) for j in (0, 1)]
        vstg = [mk.sb(f"B0_vstg{j}_{L}", (128, 1040), F32R) for j in (0, 1)]
        brow = mk.sb(f"B0_brow_{L}", (1, 1024), F32R)
        onec = mk.sb(f"B0_onec_{L}", (128, 1), F32)

        nz = k.op("vector", "memset", onec[:], 1.0, inc=VE)
        nvo = 0
        for j in (0, 1):
            nvo = k.op("vector", "tensor_copy",
                       vstg[j][:].rearrange("p (h q) -> p h q", h=16)[:, :, 64:65],
                       onec[:].broadcast_to([128, 16, 1]),
                       waits=[(VE, nz)], lowp=True, inc=VE)
        nbv = k.op("sync", "dma_start", out=brow[:],
                   in_=dram["l0_bv_row" if L == 0 else "l1_bv_row"], inc=LD)

        h_bank_sc = [0, 0, 0, 0]
        h_vbank = [0, 0]
        h_stg = [0, 0]
        h_vstg = [nvo, nvo]
        h_vstg_ld = [0, 0]
        h_wbuf = 0

        def proj_pass(which, half, nxn_ld, nw_ld):
            nonlocal h_wbuf
            tok0 = half * 2048
            res = []
            if which in ("q", "k0"):
                outsT, nk, bias_col, scale = (dram["qT"], ND, fx["bcol"][:, 32:40], None) \
                    if which == "q" else (dram["kT"], ND, fx["bcol"][:, 8:16], None)
                for o in range(ND):
                    for n in range(4):
                        bank = (o * 4 + n) % 4
                        zone = PS[:, bank * 512:(bank + 1) * 512]
                        nm = 0
                        for c in range(ND):
                            w0 = []
                            if c == 0:
                                w0 = [(SC, h_bank_sc[bank]), (MM, 0)]
                                w0.append((LD, max(nxn_ld, nw_ld)))
                            nm = k.op("tensor", "matmul", zone,
                                      wbuf[:, c * 1024 + o * 128:c * 1024 + (o + 1) * 128],
                                      xnTh[:, c * 2048 + n * 512:c * 2048 + (n + 1) * 512],
                                      start=(c == 0), stop=(c == ND - 1), waits=w0, inc=MM)
                        j = n % 2
                        if which == "q":
                            sc_kw = dict(scale=fx["bcol"][:, 24 + o:25 + o]) if L == 0 else dict(scale=SCALE)
                        else:
                            sc_kw = {}
                        wq_ = [(MM, nm), (LD, h_stg[j])]
                        if prep and "qs" in prep:
                            wq_.append((k.s("VE"), prep["qs"]))
                        nev = k.op("scalar", "activation", out=stg[j][:, 0:512],
                                   in_=zone, func=AF.Identity, bias=bias_col[:, o:o + 1],
                                   waits=wq_, inc=SC, **sc_kw)
                        h_bank_sc[bank] = nev
                        h_stg[j] = k.op("sync", "dma_start",
                                        out=outsT[o * 128:(o + 1) * 128, tok0 + n * 512:tok0 + (n + 1) * 512],
                                        in_=stg[j][:, 0:512], waits=[(SC, nev)], inc=LD)
            elif which == "lat":
                for o in range(2):
                    for n in range(4):
                        bank = (o * 4 + n) % 4
                        zone = PS[:, bank * 512:(bank + 1) * 512]
                        nm = 0
                        for c in range(ND):
                            w0 = [(SC, h_bank_sc[bank]), (LD, max(nxn_ld, nw_ld))] if c == 0 else []
                            nm = k.op("tensor", "matmul", zone,
                                      wbuf[:, c * 256 + o * 128:c * 256 + (o + 1) * 128],
                                      xnTh[:, c * 2048 + n * 512:c * 2048 + (n + 1) * 512],
                                      start=(c == 0), stop=(c == ND - 1), waits=w0, inc=MM)
                        nev = k.op("scalar", "activation", out=latsb[:, o * 2048 + n * 512:o * 2048 + (n + 1) * 512],
                                   in_=zone, func=AF.Identity, bias=fx["bcol"][:, 16 + o:17 + o],
                                   waits=[(MM, nm)], inc=SC)
                        h_bank_sc[bank] = nev
                        res.append(nev)
            elif which == "k1":
                for o in range(ND):
                    for n in range(4):
                        bank = (o * 4 + n) % 4
                        zone = PS[:, bank * 512:(bank + 1) * 512]
                        nm = 0
                        for c in range(2):
                            w0 = [(SC, h_bank_sc[bank]), (LD, nw_ld), (SC, nxn_ld)] if c == 0 else []
                            nm = k.op("tensor", "matmul", zone,
                                      wbuf[:, c * 1024 + o * 128:c * 1024 + (o + 1) * 128],
                                      latsb[:, c * 2048 + n * 512:c * 2048 + (n + 1) * 512],
                                      start=(c == 0), stop=(c == 1), waits=w0, inc=MM)
                        j = n % 2
                        nev = k.op("scalar", "activation", out=stg[j][:, 0:512],
                                   in_=zone, func=AF.Identity, bias=fx["bcol"][:, 8 + o:9 + o],
                                   waits=[(MM, nm), (LD, h_stg[j])], inc=SC)
                        h_bank_sc[bank] = nev
                        h_stg[j] = k.op("sync", "dma_start",
                                        out=dram["kT"][o * 128:(o + 1) * 128, tok0 + n * 512:tok0 + (n + 1) * 512],
                                        in_=stg[j][:, 0:512], waits=[(SC, nev)], inc=LD)
            elif which in ("v0", "v1"):
                nlat = nxn_ld
                for tt in range(16):
                    z = tt % 2
                    zone = PS[:, (4 + 2 * z) * 512:(6 + 2 * z) * 512]
                    nm = 0
                    ncon = ND if which == "v0" else 2
                    lhs_src = xnTh if which == "v0" else latsb
                    for fh in range(2):
                        for c in range(ncon):
                            w0 = []
                            if fh == 0 and c == 0:
                                w0 = [(VE, h_vbank[z]), (LD, max(nw_ld, nbv))]
                                if which == "v0":
                                    w0.append((LD, nxn_ld))
                                else:
                                    w0.append((SC, nlat))
                            nm = k.op("tensor", "matmul",
                                      zone[:, fh * 512:(fh + 1) * 512],
                                      lhs_src[:, c * 2048 + tt * 128:c * 2048 + (tt + 1) * 128],
                                      wbuf[:, c * 1024 + fh * 512:c * 1024 + (fh + 1) * 512],
                                      start=(c == 0), stop=False, waits=w0, inc=MM)
                        nm = k.op("tensor", "matmul", zone[:, fh * 512:(fh + 1) * 512],
                                  fx["ones_row"][:], brow[0:1, fh * 512:(fh + 1) * 512],
                                  start=False, stop=True, inc=MM)
                    j = tt % 2
                    nev = 0
                    for fh in range(2):
                        nev = k.op("vector", "tensor_copy",
                                   vstg[j][:].rearrange("p (h q) -> p h q", h=16)[:, fh * 8:(fh + 1) * 8, 0:64],
                                   zone[:, fh * 512:(fh + 1) * 512].rearrange("p (h q) -> p h q", h=8),
                                   waits=[(MM, nm), (LD, h_vstg_ld[j])] if fh == 0 else (),
                                   lowp=True, inc=VE)
                    h_vbank[z] = nev
                    h_vstg_ld[j] = k.op("sync", "dma_start",
                                        out=dram["vtok"][tok0 + tt * 128:tok0 + (tt + 1) * 128, :],
                                        in_=vstg[j][:], waits=[(VE, nev)], inc=LD)

        for half in (0, 1):
            tok0 = half * 2048
            nxn = 0
            for c in range(ND):
                nxn = k.op("sync", "dma_start", out=xnTh[:, c * 2048:(c + 1) * 2048],
                           in_=dram["xnT"][c * 128:(c + 1) * 128, tok0:tok0 + 2048],
                           waits=[(MM, h_wbuf)] if c == 0 else (), inc=LD)
            if L == 0:
                for wname, which in (("l0_wqT", "q"), ("l0_wkT", "k0"), ("l0_wvT", "v0")):
                    nw = k.op("sync", "dma_start",
                              out=wbuf[:].rearrange("p (c f) -> p c f", c=ND),
                              in_=dram[wname].rearrange("(c p) f -> p c f", p=128),
                              waits=[(MM, h_wbuf)], inc=LD)
                    proj_pass(which, half, nxn, nw)
                    h_wbuf = MM.n
            else:
                nw = k.op("sync", "dma_start",
                          out=wbuf[:].rearrange("p (c f) -> p c f", c=ND),
                          in_=dram["l1_wq"].rearrange("(c p) f -> p c f", p=128),
                          waits=[(MM, h_wbuf)], inc=LD)
                proj_pass("q", half, nxn, nw)
                h_wbuf = MM.n
                nw = k.op("sync", "dma_start",
                          out=wbuf[:, 0:2048].rearrange("p (c f) -> p c f", c=ND),
                          in_=dram["l1_wl"].rearrange("(c p) f -> p c f", p=128),
                          waits=[(MM, h_wbuf)], inc=LD)
                proj_pass("lat", half, nxn, nw)
                h_wbuf = MM.n
                nlat_done = SC.n
                nw = k.op("sync", "dma_start",
                          out=wbuf[:, 0:2048].rearrange("p (c f) -> p c f", c=2),
                          in_=dram["l1_wk"].rearrange("(c p) f -> p c f", p=128),
                          waits=[(MM, h_wbuf)], inc=LD)
                proj_pass("k1", half, nlat_done, nw)
                h_wbuf = MM.n
                nw = k.op("sync", "dma_start",
                          out=wbuf[:, 0:2048].rearrange("p (c f) -> p c f", c=2),
                          in_=dram["l1_wv"].rearrange("(c p) f -> p c f", p=128),
                          waits=[(MM, h_wbuf)], inc=LD)
                proj_pass("v1", half, nlat_done, nw)
                h_wbuf = MM.n
        k.barrier()
    mk.cur_ctx = None



def phase_B(mk, L):
    k, nc, fx, dram = mk.k, mk.nc, mk.fixed, mk.dram
    LD, GP, MM, VE, SC = (k.s(x) for x in ("LD", "GP", "MM", "VE", "SC"))
    PS = mk.PS
    from contextlib import ExitStack
    ctx = ExitStack()
    mk.cur_ctx = ctx
    with ctx:
        qp = [mk.sb(f"B_qp{j}_{L}", (128, T), F32R) for j in (0, 1)]
        kp = [mk.sb(f"B_kp{j}_{L}", (128, T), F32R) for j in (0, 1)]
        vp = [mk.sb(f"B_vp{j}_{L}", (128, 32 * 130), F32R) for j in (0, 1)]
        wob = [mk.sb(f"B_wob{j}_{L}", (128, 1024), F32R) for j in (0, 1)]
        sqb = mk.sb(f"B_sqb_{L}", (128, T), F32R)
        pb = [mk.sb(f"B_pb{j}_{L}", (128, 1024), F32R) for j in (0, 1, 2)]
        ao = mk.sb(f"B_ao_{L}", (128, T), F32R)
        binv = mk.sb(f"B_binv_{L}", (64, 512), F32)
        lrec = mk.sb(f"B_lrec_{L}", (1, 512), F32R)
        nrw = mk.sb(f"B_nrw_{L}", (1, 16), F32)
        cv = mk.sb(f"B_cv_{L}", (1, 4), F32R)
        csc = mk.sb(f"B_csc_{L}", (128, 2), F32)
        borow = mk.sb(f"B_borow_{L}", (1, 1024), F32R)
        stgB = [mk.sb(f"B_stgB{j}_{L}", (128, 1024), F32) for j in (0, 1)]

        nbo = k.op("sync", "dma_start", out=borow[:],
                   in_=dram["l0_bo_row" if L == 0 else "l1_bo_row"], inc=LD)
        woT = dram["l0_woT"] if L == 0 else dram["l1_wo"]

        h_pair_rd = [0, 0]
        h_sacc = [0] * NT
        h_wob_rd = [0, 0]
        h_stgB = [0, 0]
        n_loads = [0, 0]
        n_wob = [0, 0]
        h_sq_mm = 0
        h_pv_cons = 0
        h_bc_cons = 0
        h_sc_banks = [0] * 6
        h_pb_rd = [0, 0, 0]
        h_lrec_rd = 0

        for p in range(NP):
            j = p % 2
            n1 = k.op("sync", "dma_start", out=qp[j][:], in_=dram["qT"][p * 128:(p + 1) * 128, :],
                      waits=[(MM, h_pair_rd[j])], inc=LD)
            n2 = k.op("sync", "dma_start", out=kp[j][:], in_=dram["kT"][p * 128:(p + 1) * 128, :],
                      inc=LD)
            n3 = k.op("sync", "dma_start",
                      out=vp[j][:].rearrange("q (c u) -> q c u", c=32),
                      in_=dram["vtok"][:, p * 130:(p + 1) * 130].rearrange("(c q) u -> q c u", q=128),
                      inc=LD)
            n_loads[j] = n3
            n_wob[j] = k.op("sync", "dma_start", out=wob[j][:], in_=woT[p * 128:(p + 1) * 128, :],
                            waits=[(MM, h_wob_rd[j])], inc=LD)
            for hl in range(2):
                nsq = k.op("scalar", "activation", out=sqb[:], in_=kp[j][:].bitcast(F32) if hl == 0 else qp[j][:].bitcast(F32),
                           func=AF.Square, waits=[(LD, n_loads[j]), (MM, h_sq_mm)], inc=SC)
                for hh in range(2):
                    hr0 = hh * 64
                    nmn = 0
                    for n in range(8):
                        nmn = k.op("tensor", "matmul", PS[0:1, n * 512:n * 512 + 512],
                                   fx["ones_col"][hr0:hr0 + 64, :],
                                   sqb[hr0:hr0 + 64, n * 512:(n + 1) * 512],
                                   start=True, stop=True,
                                   waits=[(SC, nsq), (VE, VE.n)] if n == 0 else (), inc=MM)
                    k.op("vector", "tensor_reduce", out=nrw[0:1, hl * 4 + hh:hl * 4 + hh + 1],
                         in_=PS[0:1, 0:4096], op=ALU.max, axis=AXX,
                         waits=[(MM, nmn)], inc=VE)
                    k.wait("tensor", VE, VE.n)
                h_sq_mm = MM.n
            nprod = 0
            for hh in range(2):
                nprod = k.op("vector", "tensor_tensor", out=nrw[0:1, 8 + hh:9 + hh],
                             in0=nrw[0:1, hh:hh + 1], in1=nrw[0:1, 4 + hh:5 + hh],
                             op=ALU.mult, waits=[(VE, VE.n)] if hh == 0 else (), inc=VE)
            nln = k.op("scalar", "activation", out=nrw[0:1, 10:12], in_=nrw[0:1, 8:10],
                       func=AF.Ln, bias=fx["consts"][0:1, 1:2], waits=[(VE, nprod)], inc=SC)
            k.wait("scalar", SC, nln)
            ncv = k.op("scalar", "activation", out=cv[:, 0:2], in_=nrw[0:1, 10:12],
                       func=AF.Exp, scale=0.5, inc=SC)
            ncsc = 0
            for hh in range(2):
                nbc = k.op("tensor", "matmul", PS[:, 3072 + hh * 2: 3072 + hh * 2 + 2],
                           fx["ones_row"][:], cv[0:1, 0:2],
                           start=True, stop=True,
                           waits=[(SC, ncv), (SC, h_bc_cons)] if hh == 0 else (), inc=MM)
                ncsc = k.op("scalar", "activation", out=csc[:, hh:hh + 1],
                            in_=PS[:, 3072 + hh * 2 + hh:3072 + hh * 2 + hh + 1],
                            func=AF.Copy, scale=-1.0, waits=[(MM, nbc)], inc=SC)
            for hh in range(2):
                hr0 = hh * 64
                for n in range(8):
                    q0 = n * 512
                    npv = 0
                    for grp in range(16):
                        b2 = (grp % 3) * 2
                        nsc2 = 0
                        for s2 in range(2):
                            kc = grp * 2 + s2
                            w0 = [(SC, h_sc_banks[b2 + s2])]
                            if grp == 0 and s2 == 0:
                                w0 += [(VE, h_pv_cons), (LD, n_loads[j]), (SC, ncsc)]
                            nsc2 = k.op("tensor", "matmul",
                                        PS[:, (b2 + s2) * 512:(b2 + s2 + 1) * 512],
                                        kp[j][hr0:hr0 + 64, kc * 128:(kc + 1) * 128],
                                        qp[j][hr0:hr0 + 64, q0:q0 + 512],
                                        start=True, stop=True, waits=w0, inc=MM)
                        slot = grp % 3
                        nexp = k.op("scalar", "activation", out=pb[slot][:],
                                    in_=PS[:, b2 * 512:(b2 + 2) * 512], func=AF.Exp,
                                    bias=csc[:, hh:hh + 1],
                                    waits=[(MM, nsc2), (MM, h_pb_rd[slot])], inc=SC)
                        h_sc_banks[b2] = nexp
                        h_sc_banks[b2 + 1] = nexp
                        for s2 in range(2):
                            kc = grp * 2 + s2
                            npv = k.op("tensor", "matmul", PS[0:65, 3584:4096],
                                       vp[j][:, kc * 130 + hh * 65:kc * 130 + (hh + 1) * 65],
                                       pb[slot][:, s2 * 512:(s2 + 1) * 512],
                                       start=(grp == 0 and s2 == 0), stop=(grp == 15 and s2 == 1),
                                       waits=[(SC, nexp)] if s2 == 0 else (), inc=MM)
                        h_pb_rd[slot] = npv
                    nlr = k.op("vector", "reciprocal", lrec[:], PS[64:65, 3584:4096],
                               waits=[(MM, max(npv, h_lrec_rd))], lowp=True, inc=VE)
                    nbc = k.op("tensor", "matmul", PS[0:64, 3072:3584],
                               fx["ones_row"][0:1, 0:64], lrec[:],
                               start=True, stop=True, waits=[(VE, nlr), (SC, h_bc_cons)], inc=MM)
                    h_lrec_rd = nbc
                    nbi = k.op("scalar", "activation", out=binv[:], in_=PS[0:64, 3072:3584],
                               func=AF.Copy, waits=[(MM, nbc)], inc=SC)
                    h_bc_cons = nbi
                    h_pv_cons = k.op("vector", "tensor_tensor", out=ao[hr0:hr0 + 64, q0:q0 + 512],
                                     in0=PS[0:64, 3584:4096], in1=binv[:], op=ALU.mult,
                                     waits=[(SC, nbi)], lowp=True, inc=VE)
            for tt in range(NT):
                z = tt % 2
                zone = PS[:, z * 1024:(z + 1) * 1024]
                nm = 0
                for fh in range(2):
                    w0 = []
                    if fh == 0:
                        w0 = [(VE, h_pv_cons), (LD, n_wob[j]), (SC, h_sc_banks[z * 2]), (SC, h_sc_banks[z * 2 + 1])]
                    nm = k.op("tensor", "matmul", zone[:, fh * 512:(fh + 1) * 512],
                              ao[:, tt * 128:(tt + 1) * 128],
                              wob[j][:, fh * 512:(fh + 1) * 512],
                              start=True, stop=(p != 0), waits=w0, inc=MM)
                    if p == 0:
                        nm = k.op("tensor", "matmul", zone[:, fh * 512:(fh + 1) * 512],
                                  fx["ones_row"][:], borow[0:1, fh * 512:(fh + 1) * 512],
                                  start=False, stop=True,
                                  waits=[(LD, nbo)] if fh == 0 and tt == 0 else (), inc=MM)
                nev = k.op("scalar", "activation", out=stgB[z][:], in_=zone, func=AF.Copy,
                           waits=[(MM, nm), (GP, h_stgB[z])], inc=SC)
                h_sc_banks[z * 2] = nev
                h_sc_banks[z * 2 + 1] = nev
                kw = {} if p == 0 else dict(accum_op=ALU.add)
                h_stgB[z] = k.op("gpsimd", "dma_start", out=dram["sacc"][tt * 128:(tt + 1) * 128, :],
                                 in_=stgB[z][:], waits=[(SC, nev), (GP, h_sacc[tt])], inc=GP, **kw)
                h_sacc[tt] = h_stgB[z]
            h_pair_rd[j] = MM.n
            h_wob_rd[j] = MM.n
        k.barrier()
    mk.cur_ctx = None


def emit_ln_tile(mk, src_ap, stats, so, scrS, waits_in, eps_col):
    k = mk.k
    VE, SC = k.s("VE"), k.s("SC")
    ALUa = ALU.add
    nsq0 = k.op("scalar", "activation", out=scrS[:], in_=src_ap, func=AF.Square,
                waits=list(waits_in) + [(VE, mk._ln_scr_rd)], inc=SC)
    nred = 0
    for q_ in range(4):
        sl = slice(q_ * 256, (q_ + 1) * 256)
        k.op("vector", "tensor_reduce", out=stats[:, so + 8 + q_:so + 9 + q_],
             in_=src_ap[:, sl] if hasattr(src_ap, "shape") else src_ap,
             op=ALUa, axis=AXX, waits=list(waits_in) if q_ == 0 else (), inc=VE)
        nred = k.op("vector", "tensor_reduce", out=stats[:, so + 12 + q_:so + 13 + q_],
                    in_=scrS[:, sl], op=ALUa, axis=AXX,
                    waits=[(SC, nsq0)] if q_ == 0 else (), inc=VE)
    mk._ln_scr_rd = nred
    k.wait("vector", VE, nred)
    k.op("vector", "tensor_tensor", out=stats[:, so + 16:so + 18],
         in0=stats[:, so + 8:so + 10], in1=stats[:, so + 10:so + 12], op=ALUa, inc=VE)
    nl = k.op("vector", "tensor_tensor", out=stats[:, so + 18:so + 20],
              in0=stats[:, so + 12:so + 14], in1=stats[:, so + 14:so + 16], op=ALUa, inc=VE)
    k.wait("vector", VE, nl)
    k.op("vector", "tensor_tensor", out=stats[:, so:so + 1],
         in0=stats[:, so + 16:so + 17], in1=stats[:, so + 17:so + 18], op=ALUa, inc=VE)
    nl = k.op("vector", "tensor_tensor", out=stats[:, so + 1:so + 2],
              in0=stats[:, so + 18:so + 19], in1=stats[:, so + 19:so + 20], op=ALUa, inc=VE)
    k.wait("vector", VE, nl)
    nmu = k.op("vector", "tensor_scalar", out=stats[:, so + 2:so + 3],
               in0=stats[:, so:so + 1], scalar1=1.0 / D, scalar2=None, op0=ALU.mult, inc=VE)
    k.wait("vector", VE, nmu)
    nmu2 = k.op("vector", "tensor_tensor", out=stats[:, so + 3:so + 4],
                in0=stats[:, so + 2:so + 3], in1=stats[:, so + 2:so + 3], op=ALU.mult, inc=VE)
    k.wait("vector", VE, nmu2)
    nvar = k.op("vector", "scalar_tensor_tensor", out=stats[:, so + 4:so + 5],
                in0=stats[:, so + 1:so + 2], scalar=1.0 / D, in1=stats[:, so + 3:so + 4],
                op0=ALU.mult, op1=ALU.subtract, inc=VE)
    nln = k.op("scalar", "activation", out=stats[:, so + 5:so + 6], in_=stats[:, so + 4:so + 5],
               func=AF.Ln, bias=eps_col, waits=[(VE, nvar)], inc=SC)
    k.wait("scalar", SC, nln)
    nrstd = k.op("scalar", "activation", out=stats[:, so + 6:so + 7], in_=stats[:, so + 5:so + 6],
                 func=AF.Exp, scale=-0.5, inc=SC)
    return stats[:, so + 2:so + 3], stats[:, so + 6:so + 7], nrstd



def phase_B2(mk):
    k, nc, fx, dram = mk.k, mk.nc, mk.fixed, mk.dram
    LD, GP, MM, VE, SC = k.s("LD"), k.s("GP"), k.s("MM"), k.s("VE"), k.s("SC")
    SQ = k.s("SQ")
    PS = mk.PS
    from contextlib import ExitStack
    ctx = ExitStack()
    mk.cur_ctx = ctx
    mk._ln_scr_rd = 0
    with ctx:
        sa = [mk.sb(f"B2_sa{j}", (128, D), F32) for j in (0, 1)]
        xnl = [mk.sb(f"B2_xn{j}", (128, D), F32) for j in (0, 1)]
        yt = [mk.sb(f"B2_y{j}", (128, D), F32) for j in (0, 1)]
        st_ = [mk.sb(f"B2_s{j}", (128, D), F32) for j in (0, 1)]
        stats = mk.sb("B2_stats", (128, 48), F32)
        scrS = mk.sb("B2_scrS", (128, D), F32)
        gB = mk.sb("B2_gB", (128, D), F32)
        bB = mk.sb("B2_bB", (128, D), F32)
        grow = mk.sb("B2_grow", (1, 2 * D), F32R)

        ng = k.op("sync", "dma_start", out=grow[0:1, 0:D], in_=dram["l0_gamma_row"], inc=LD)
        ng = k.op("sync", "dma_start", out=grow[0:1, D:2 * D], in_=dram["l0_beta_row"], inc=LD)
        nb = 0
        for which, dst in ((0, gB), (1, bB)):
            for fh in range(2):
                w0 = []
                if fh == 0:
                    w0 = [(LD, ng)] if which == 0 else [(SC, nb)]
                nm = k.op("tensor", "matmul", PS[:, fh * 512:(fh + 1) * 512],
                          fx["ones_row"][:], grow[0:1, which * D + fh * 512:which * D + (fh + 1) * 512],
                          start=True, stop=True, waits=w0, inc=MM)
            nb = k.op("scalar", "activation", out=dst[:], in_=PS[:, 0:1024], func=AF.Copy,
                      waits=[(MM, nm)], inc=SC)
        h_y = [0, 0]
        h_s = [0, 0]
        for i in range(NT):
            p = i % 2
            so = p * 24
            r0 = i * 128
            n1 = k.op("sync", "dma_start", out=sa[p][:], in_=dram["sacc"][r0:r0 + 128, :],
                      waits=[(VE, h_y[p]), (GP, k.s("GP").n)] if i < 2 else [(VE, h_y[p])], inc=LD)
            n2 = k.op("sync", "dma_start", out=xnl[p][:], in_=dram["xn_tok"][r0:r0 + 128, :],
                      inc=LD)
            ny = k.op("vector", "tensor_tensor", out=yt[p][:], in0=sa[p][:], in1=xnl[p][:],
                      op=ALU.add, waits=[(LD, n2), (SQ, h_s[p])], inc=VE)
            h_y[p] = ny
            mu, rstd, nrstd = emit_ln_tile(mk, yt[p][:], stats, so, scrS,
                                           [(VE, ny)], fx["consts"][:, 0:1])
            nsn = k.op("vector", "tensor_scalar", out=scrS[:], in0=yt[p][:],
                       scalar1=mu, scalar2=rstd, op0=ALU.subtract, op1=ALU.mult,
                       waits=[(SC, nrstd)], inc=VE)
            mk._ln_scr_rd = nsn
            k.wait("vector", VE, nsn)
            ngm = k.op("vector", "tensor_tensor", out=yt[p][:], in0=scrS[:], in1=gB[:],
                       op=ALU.mult, waits=[(SC, nb)] if i == 0 else (), inc=VE)
            mk._ln_scr_rd = ngm
            k.wait("vector", VE, ngm)
            nsf = k.op("vector", "tensor_tensor", out=st_[p][:], in0=yt[p][:], in1=bB[:],
                       op=ALU.add, inc=VE)
            h_s[p] = k.op("scalar", "dma_start", out=dram["sfin"][r0:r0 + 128, :],
                          in_=st_[p][:], waits=[(VE, nsf)], inc=SQ, inc_n=16)
        k.barrier()
    mk.cur_ctx = None



def phase_C(mk):
    k, nc, fx, dram = mk.k, mk.nc, mk.fixed, mk.dram
    LD, GP, MM, VE, SC = k.s("LD"), k.s("GP"), k.s("MM"), k.s("VE"), k.s("SC")
    PS = mk.PS
    from contextlib import ExitStack
    ctx = ExitStack()
    mk.cur_ctx = ctx
    with ctx:
        w1b = mk.sb("C_w1", (128, ND * HID0), F32R)
        w2b = mk.sb("C_w2", (128, (HID0 // 128) * D), F32R)
        h1b = mk.sb("C_h1", (128, (HID0 // 128) * 512), F32R)
        xch = mk.sb("C_xch", (128, ND * 512), F32R)
        stg = [mk.sb(f"C_stg{j}", (128, D), F32) for j in (0, 1)]
        bcol = mk.sb("C_bcol", (128, 16), F32)
        b2row = mk.sb("C_b2row", (1, D), F32R)

        h_w_rd = 0
        h_x_rd = 0
        h_h1_rd = 0
        h_stg = [0, 0]
        h_zone_sc = [0, 0]
        h_gacc = [0] * NT
        nh1_sc = 0
        for em in range(-1, E0):
            if em < 0:
                w1src, w2src = dram["l0_sw1"], dram["l0_sw2"]
                b1src, b2src = dram["l0_sb1"], dram["l0_sb2_row"]
            else:
                w1src, w2src = dram["l0_ew1"][em], dram["l0_ew2"][em]
                b1src, b2src = dram["l0_eb1"][em], dram["l0_eb2"][em]
            nw = k.op("sync", "dma_start", out=w1b[:].rearrange("p (c f) -> p c f", c=ND),
                      in_=w1src.rearrange("(c p) f -> p c f", p=128),
                      waits=[(MM, h_w_rd)], inc=LD)
            nw = k.op("sync", "dma_start", out=w2b[:].rearrange("p (c f) -> p c f", c=HID0 // 128),
                      in_=w2src.rearrange("(c p) f -> p c f", p=128), inc=LD)
            nw = k.op("sync", "dma_start", out=bcol[:], in_=b1src, inc=LD)
            nw = k.op("sync", "dma_start", out=b2row[:], in_=b2src, inc=LD)
            for t5 in range(8):
                tok0 = t5 * 512
                nx = 0
                for c in range(ND):
                    nx = k.op("sync", "dma_start", out=xch[:, c * 512:(c + 1) * 512],
                              in_=dram["xnT"][c * 128:(c + 1) * 128, tok0:tok0 + 512],
                              waits=[(MM, h_x_rd)] if c == 0 else (), inc=LD)
                for hc in range(HID0 // 128):
                    bank = hc % 2
                    zone = PS[:, bank * 512:(bank + 1) * 512]
                    nm = 0
                    for c in range(ND):
                        w0 = []
                        if c == 0:
                            w0 = [(SC, nh1_sc), (LD, max(nw, nx)), (MM, 0)]
                        nm = k.op("tensor", "matmul", zone,
                                  w1b[:, c * HID0 + hc * 128:c * HID0 + (hc + 1) * 128],
                                  xch[:, c * 512:(c + 1) * 512],
                                  start=(c == 0), stop=(c == ND - 1), waits=w0, inc=MM)
                    nh1_sc = k.op("scalar", "activation", out=h1b[:, hc * 512:(hc + 1) * 512],
                                  in_=zone, func=AF.Gelu, bias=bcol[:, hc:hc + 1],
                                  waits=[(MM, max(nm, h_h1_rd))], inc=SC)
                h_x_rd = MM.n
                for tt in range(4):
                    z = tt % 2
                    zone = PS[:, (2 + 2 * z) * 512:(4 + 2 * z) * 512]
                    nm = 0
                    for fh in range(2):
                        for hc in range(HID0 // 128):
                            w0 = []
                            if fh == 0 and hc == 0:
                                w0 = [(SC, nh1_sc), (SC, h_zone_sc[z])]
                            nm = k.op("tensor", "matmul", zone[:, fh * 512:(fh + 1) * 512],
                                      h1b[:, hc * 512 + tt * 128:hc * 512 + tt * 128 + 128],
                                      w2b[:, hc * D + fh * 512:hc * D + (fh + 1) * 512],
                                      start=(hc == 0), stop=False, waits=w0, inc=MM)
                        nm = k.op("tensor", "matmul", zone[:, fh * 512:(fh + 1) * 512],
                                  fx["ones_row"][:], b2row[0:1, fh * 512:(fh + 1) * 512],
                                  start=False, stop=True, inc=MM)
                    h_h1_rd = nm
                    ti = t5 * 4 + tt
                    kwsc = {} if em < 0 else dict(scale=fx["W0"][:, ti * E0 + em:ti * E0 + em + 1])
                    nev = k.op("scalar", "activation", out=stg[z][:], in_=zone,
                               func=(AF.Identity if kwsc else AF.Copy),
                               waits=[(MM, nm), (GP, h_stg[z])], inc=SC, **kwsc)
                    h_zone_sc[z] = nev
                    kwa = {} if em < 0 else dict(accum_op=ALU.add)
                    h_stg[z] = k.op("gpsimd", "dma_start",
                                    out=dram["gacc"][ti * 128:(ti + 1) * 128, :],
                                    in_=stg[z][:],
                                    waits=[(SC, nev), (GP, h_gacc[ti])], inc=GP, **kwa)
                    h_gacc[ti] = h_stg[z]
            h_w_rd = MM.n
        k.barrier()
    mk.cur_ctx = None



def phase_C1(mk):
    k, nc, fx, dram = mk.k, mk.nc, mk.fixed, mk.dram
    LD, GP, MM, VE, SC = k.s("LD"), k.s("GP"), k.s("MM"), k.s("VE"), k.s("SC")
    PS = mk.PS
    NH = HID1 // 128
    from contextlib import ExitStack
    ctx = ExitStack()
    mk.cur_ctx = ctx
    with ctx:
        wgb_ = [mk.sb(f"C1_wg{j}", (128, ND * HID1), F32R) for j in (0, 1)]
        wub_ = [mk.sb(f"C1_wu{j}", (128, ND * HID1), F32R) for j in (0, 1)]
        wdb_ = [mk.sb(f"C1_wd{j}", (128, NH * D), F32R) for j in (0, 1)]
        gsb = mk.sb("C1_gs", (128, 512), F32)
        msb = mk.sb("C1_m", (128, NH * 512), F32R)
        xch_ = [mk.sb(f"C1_xch{j}", (128, ND * 512), F32R) for j in (0, 1)]
        stg = [mk.sb(f"C1_stg{j}", (128, D), F32) for j in (0, 1)]

        h_w_rd_ = [0, 0]
        h_x_rd_ = [0, 0]
        h_x_rd = 0
        h_m_rd = 0
        h_gs_rd = 0
        h_stg = [0, 0]
        h_zone_sc = [0, 0]
        h_gacc = [0] * NT
        h_m_ve = 0
        first = True
        for g in (0, 1):
            W1 = fx[f"W1{g}"]
            for e in range(E1):
                ew = e % 2
                wgb, wub, wdb = wgb_[ew], wub_[ew], wdb_[ew]
                nw = k.op("sync", "dma_start", out=wgb[:].rearrange("p (c f) -> p c f", c=ND),
                          in_=dram[f"l1_wg{g}"][e].rearrange("(c p) f -> p c f", p=128),
                          waits=[(MM, h_w_rd_[ew])], inc=LD)
                nw = k.op("sync", "dma_start", out=wub[:].rearrange("p (c f) -> p c f", c=ND),
                          in_=dram[f"l1_wu{g}"][e].rearrange("(c p) f -> p c f", p=128), inc=LD)
                nw = k.op("sync", "dma_start", out=wdb[:].rearrange("p (c f) -> p c f", c=NH),
                          in_=dram[f"l1_wd{g}"][e].rearrange("(c p) f -> p c f", p=128), inc=LD)
                for t5 in range(8):
                    xp = t5 % 2
                    xch = xch_[xp]
                    tok0 = t5 * 512
                    nx = 0
                    for c in range(ND):
                        nx = k.op("sync", "dma_start", out=xch[:, c * 512:(c + 1) * 512],
                                  in_=dram["xnT"][c * 128:(c + 1) * 128, tok0:tok0 + 512],
                                  waits=[(MM, h_x_rd_[xp])] if c == 0 else (), inc=LD)
                    nm_ve = 0
                    for hp in range(NH):
                        nmg = 0
                        nmu_ = 0
                        for c in range(ND):
                            w0 = [(SC, SC.n), (VE, h_gs_rd), (LD, max(nw, nx))] if c == 0 else ()
                            nmg = k.op("tensor", "matmul", PS[:, 0:512],
                                       wgb[:, c * HID1 + hp * 128:c * HID1 + (hp + 1) * 128],
                                       xch[:, c * 512:(c + 1) * 512],
                                       start=(c == 0), stop=(c == ND - 1), waits=w0, inc=MM)
                        for c in range(ND):
                            w0 = [(VE, h_m_ve)] if c == 0 else ()
                            nmu_ = k.op("tensor", "matmul", PS[:, 512:1024],
                                        wub[:, c * HID1 + hp * 128:c * HID1 + (hp + 1) * 128],
                                        xch[:, c * 512:(c + 1) * 512],
                                        start=(c == 0), stop=(c == ND - 1), waits=w0, inc=MM)
                        nsi = k.op("scalar", "activation", out=gsb[:], in_=PS[:, 0:512],
                                   func=AF.Silu, waits=[(MM, nmg), (VE, h_gs_rd)], inc=SC)
                        h_m_ve = k.op("vector", "tensor_tensor",
                                      out=msb[:, hp * 512:(hp + 1) * 512],
                                      in0=PS[:, 512:1024], in1=gsb[:], op=ALU.mult,
                                      waits=[(SC, nsi), (MM, max(nmu_, h_m_rd))], lowp=True, inc=VE)
                        h_gs_rd = h_m_ve
                    h_x_rd_[xp] = MM.n
                    for tt in range(4):
                        z = tt % 2
                        zone = PS[:, (2 + 2 * z) * 512:(4 + 2 * z) * 512]
                        nm = 0
                        for fh in range(2):
                            for hc in range(NH):
                                w0 = []
                                if fh == 0 and hc == 0:
                                    w0 = [(VE, h_m_ve), (SC, h_zone_sc[z])]
                                nm = k.op("tensor", "matmul", zone[:, fh * 512:(fh + 1) * 512],
                                          msb[:, hc * 512 + tt * 128:hc * 512 + tt * 128 + 128],
                                          wdb[:, hc * D + fh * 512:hc * D + (fh + 1) * 512],
                                          start=(hc == 0), stop=(hc == NH - 1), waits=w0, inc=MM)
                        h_m_rd = nm
                        ti = t5 * 4 + tt
                        nev = k.op("scalar", "activation", out=stg[z][:], in_=zone, func=AF.Identity,
                                   scale=W1[:, ti * E1 + e:ti * E1 + e + 1],
                                   waits=[(MM, nm), (GP, h_stg[z])], inc=SC)
                        h_zone_sc[z] = nev
                        kwa = {} if first else dict(accum_op=ALU.add)
                        h_stg[z] = k.op("gpsimd", "dma_start",
                                        out=dram["gacc"][ti * 128:(ti + 1) * 128, :],
                                        in_=stg[z][:],
                                        waits=[(SC, nev), (GP, h_gacc[ti])], inc=GP, **kwa)
                        h_gacc[ti] = h_stg[z]
                    if t5 == 7:
                        first = False
                h_w_rd_[ew] = MM.n
            if g == 0:
                first = False
        k.barrier()
    mk.cur_ctx = None



def phase_D(mk, L):
    k, nc, fx, dram = mk.k, mk.nc, mk.fixed, mk.dram
    LD, GP, MM, VE, SC = k.s("LD"), k.s("GP"), k.s("MM"), k.s("VE"), k.s("SC")
    SQ = k.s("SQ")
    from contextlib import ExitStack
    ctx = ExitStack()
    mk.cur_ctx = ctx
    with ctx:
        xb = [mk.sb(f"D_x{j}_{L}", (128, D), F32) for j in (0, 1)]
        sb_ = [mk.sb(f"D_s{j}_{L}", (128, D), F32) for j in (0, 1)]
        gb = [mk.sb(f"D_g{j}_{L}", (128, D), F32) for j in (0, 1)]
        gg = [mk.sb(f"D_gg{j}_{L}", (128, D), F32) for j in (0, 1)]
        tb = [mk.sb(f"D_t{j}_{L}", (128, D), F32) for j in (0, 1)]
        ob = [mk.sb(f"D_o{j}_{L}", (128, D), F32) for j in (0, 1)]
        xsrc = dram["x0"] if L == 0 else dram["x1"]
        ssrc = dram["sfin"] if L == 0 else dram["sacc"]
        osink = dram["x1"] if L == 0 else dram["out"]
        h_t = [0, 0]
        h_no = [0, 0]
        h_odma = [0, 0]
        h_g = [0, 0]
        for i in range(NT):
            p = i % 2
            r0 = i * 128
            w0 = [(VE, h_no[p])]
            if i < 2:
                w0.append((GP, k.s("GP").n))
                w0.append((SQ, k.s("SQ").n))
            n1 = k.op("sync", "dma_start", out=xb[p][:], in_=xsrc[r0:r0 + 128, :],
                      waits=w0, inc=LD)
            n2 = k.op("sync", "dma_start", out=sb_[p][:], in_=ssrc[r0:r0 + 128, :], inc=LD)
            n3 = k.op("sync", "dma_start", out=gb[p][:], in_=dram["gacc"][r0:r0 + 128, :],
                      waits=[(SC, h_g[p])], inc=LD)
            nsi = k.op("scalar", "activation", out=gg[p][:], in_=gb[p][:], func=AF.Silu,
                       waits=[(LD, n3), (VE, h_t[p])], inc=SC)
            h_g[p] = nsi
            nt = k.op("vector", "tensor_tensor", out=tb[p][:], in0=sb_[p][:], in1=gg[p][:],
                      op=ALU.mult, waits=[(SC, nsi), (LD, n2)], inc=VE)
            h_t[p] = nt
            k.wait("vector", VE, nt)
            no = k.op("vector", "tensor_tensor", out=ob[p][:], in0=xb[p][:], in1=tb[p][:],
                      op=ALU.add, waits=[(LD, h_odma[p])], inc=VE)
            h_no[p] = no
            h_odma[p] = k.op("sync", "dma_start", out=osink[r0:r0 + 128, :], in_=ob[p][:],
                             waits=[(VE, no)], inc=LD)
        k.barrier()
    mk.cur_ctx = None



def kernel(**inputs):
    from concourse.bass_utils import run_bass_kernel_spmd
    shared = host_prepare(inputs)
    x = np.ascontiguousarray(np.asarray(inputs["L0_attn_x"], dtype=np.float32))
    B = x.shape[0]
    nc = build(layers=(0, 1), dbg=False)
    in_maps = []
    for b in range(B):
        im = dict(shared)
        im["x0"] = np.ascontiguousarray(x[b])
        in_maps.append(im)
    res = run_bass_kernel_spmd(nc, in_maps, core_ids=list(range(B)))
    out = np.stack([np.asarray(res.results[b]["out"], dtype=np.float32) for b in range(B)], axis=0)
    return out
